# version 05
# baseline (speedup 1.0000x reference)
import math
import contextlib
import traceback as _tb
import numpy as np

_orig_ffs = _tb.StackSummary.format_frame_summary

def _safe_ffs(self, frame_summary, **kw):
    try:
        return _orig_ffs(self, frame_summary, **kw)
    except Exception:
        return f"  File {frame_summary.filename}, line {frame_summary.lineno}\n"

_tb.StackSummary.format_frame_summary = _safe_ffs
import concourse.bass as bass
import concourse.mybir as mybir
from concourse.bass_utils import run_bass_kernel_spmd

F32 = mybir.dt.float32
BF16 = mybir.dt.bfloat16
AF = mybir.ActivationFunctionType
ALU = mybir.AluOpType
AX = mybir.AxisListType

D = 4096
BATCH = 2
SEQ = 4096
NCORES = 8
RPB = 4
TOK = 1024
KC = 32
H = 32
HPC = 8
DK = 128
NQT = 32
SCALE_A = float(192.0 ** -0.5)
SCALE_B = float(128.0 ** -0.5)
MASKV = -1000.0
LORA_SCALE = 4.0

STAGE = 4
SEQ_MODE = False
BARRIER_ENG = ""


ENG_SEM = {"t": "t", "v": "v", "s": "s", "g": "gc", "y": "yc"}


class Sched:

    NROT = {"yd": 16, "gd": 8, "cc": 4}

    def __init__(self):
        self.steps = {"t": [], "v": [], "s": [], "g": [], "y": []}
        self.cnt = {}
        self.pages = {}
        self.n_instr = 0
        self.rot = {"yd": 0, "gd": 0, "cc": 0}

    def _touch(self, key):
        if key not in self.pages:
            self.pages[key] = {"w": {}, "r": {}}
        return self.pages[key]

    def step(self, eng, fn, reads=(), writes=(), sem=None, inc=1, ni=1):
        if sem is None:
            sem = ENG_SEM[eng]
        elif sem in self.rot:
            n = self.rot[sem]
            self.rot[sem] += 1
            sem = f"{sem}{n % self.NROT[sem]}"
        waits = {}
        if SEQ_MODE or eng in BARRIER_ENG:
            for s_, c_ in self.cnt.items():
                if c_ > 0:
                    waits[s_] = c_
        own = ENG_SEM[eng] if eng == "t" else None
        for k in reads:
            pg = self._touch(k)
            for s_, c_ in pg["w"].items():
                if s_ != own:
                    waits[s_] = max(waits.get(s_, 0), c_)
        for k in writes:
            pg = self._touch(k)
            for s_, c_ in pg["w"].items():
                if s_ != own:
                    waits[s_] = max(waits.get(s_, 0), c_)
            for s_, c_ in pg["r"].items():
                if s_ != own:
                    waits[s_] = max(waits.get(s_, 0), c_)
        new_cnt = self.cnt.get(sem, 0) + inc
        self.cnt[sem] = new_cnt
        for k in reads:
            pg = self._touch(k)
            pg["r"][sem] = new_cnt
        for k in writes:
            pg = self._touch(k)
            pg["w"] = {sem: new_cnt}
            pg["r"] = {}
        self.steps[eng].append((dict(waits), fn, sem, inc))
        self.n_instr += ni

    def replay(self, eng_name, eng, sems):
        waited = {}
        for waits, fn, sem, inc in self.steps[eng_name]:
            for s_, c_ in sorted(waits.items()):
                if waited.get(s_, 0) < c_:
                    eng.wait_ge(sems[s_], c_)
                    waited[s_] = c_
            last = fn(eng)
            if inc:
                if isinstance(last, list):
                    for ins_ in last:
                        ins_.then_inc(sems[sem], 16)
                else:
                    last.then_inc(sems[sem], inc)

    def final_wait(self, eng_name, prefix):
        waits = {s_: c_ for s_, c_ in self.cnt.items()
                 if s_.startswith(prefix) and c_ > 0}
        self.steps[eng_name].append((waits, lambda e: e.nop(), ENG_SEM[eng_name], 0))



def build_nc(stage=STAGE):
    nc = bass.Bass("TRN2", target_bir_lowering=False, debug=False)
    sch = Sched()

    dt_in = {}

    def din(name, shape):
        t = nc.dram_tensor(name, shape, F32, kind="ExternalInput")
        dt_in[name] = t
        return t

    x_in = din("x_in", [128, KC * TOK])
    wq_a = din("wq_a", [KC, 128, D])
    wk_a = din("wk_a", [KC, 128, D])
    wv_a = din("wv_a", [KC, 128, D])
    wo_a = din("wo_a", [KC, 128, D])
    wq_b = din("wq_b", [KC, 128, D])
    wk_b = din("wk_b", [1, 128, D])
    wv_b = din("wv_b", [1, 128, D])
    wo_b = din("wo_b", [KC, 128, D])
    rtr = din("rtr", [128, KC * 4])
    a_pre = din("a_pre", [128, KC * 64])
    b_pre = din("b_pre", [64, D])
    ident_in = din("ident_in", [128, 128])
    ones_in = din("ones_in", [128, 1])
    onesr_in = din("onesr_in", [1, 128])
    dmask_in = din("dmask_in", [128, 128])
    e4_in = din("e4_in", [4, 64])
    eps_in = din("eps_in", [128, 1])

    out_t = nc.dram_tensor("out", [128, KC * TOK], F32, kind="ExternalOutput")
    dbg_t = None
    if stage in (0, 0.5):
        dbg_t = nc.dram_tensor("dbg", [3 * RPB * TOK, TOK], F32, kind="ExternalOutput")
    if stage == 0.8:
        dbg_t = nc.dram_tensor("dbg", [HPC * DK, SEQ], F32, kind="ExternalOutput")
    if stage == 1.5:
        dbg_t = nc.dram_tensor("dbg", [260, TOK], F32, kind="ExternalOutput")

    qkv_in = nc.dram_tensor("qkv_in", [3 * D, TOK], BF16)
    qkv_g = nc.dram_tensor("qkv_g", [RPB * 3 * D, TOK], BF16)
    part_g = [nc.dram_tensor(f"part_g{i}", [RPB * D, TOK], BF16) for i in range(3)]
    o_in = nc.dram_tensor("o_in", [HPC * DK, SEQ], BF16)
    o_g = nc.dram_tensor("o_g", [RPB * HPC * DK, SEQ], BF16)
    kvb_in = nc.dram_tensor("kvb_in", [DK + TOK, max(TOK, DK)], BF16)
    kvb_g = nc.dram_tensor("kvb_g", [RPB * (DK + TOK), max(TOK, DK)], BF16)
    q_in_b = nc.dram_tensor("q_in_b", [D, TOK], BF16)
    q_g_b = nc.dram_tensor("q_g_b", [RPB * D, TOK], BF16)
    qkv_m = nc.dram_tensor("qkv_m", [RPB * 3 * TOK, TOK], BF16)
    qb_m = nc.dram_tensor("qb_m", [RPB * TOK, TOK], BF16)
    o_loc = nc.dram_tensor("o_loc", [RPB * HPC * DK, TOK], BF16)

    RG = [[0, 1, 2, 3], [4, 5, 6, 7]]

    _sym_cache = {}

    def sym_off(e, const, coef):
        k = (id(e), const, coef)
        if k not in _sym_cache:
            kp = (id(e), "pid")
            if kp not in _sym_cache:
                _sym_cache[kp] = e.partition_id() % 4
            _sym_cache[k] = _sym_cache[kp] * coef + const
        return _sym_cache[k]

    with contextlib.ExitStack() as st:
        E = st.enter_context

        sems = {}
        for s_ in ["t", "v", "s", "gc", "yc"]:
            sems[s_] = E(nc.semaphore("sem_" + s_))
        for base, nr in Sched.NROT.items():
            for i_ in range(nr):
                sems[f"{base}{i_}"] = E(nc.semaphore(f"sem_{base}{i_}"))

        x_sb = E(nc.sbuf_tensor("x_sb", [128, KC * TOK], BF16))
        a1 = E(nc.sbuf_tensor("a1", [128, 48128], BF16))
        ident_bf = E(nc.sbuf_tensor("ident_bf", [128, 128], BF16))
        ident_f = E(nc.sbuf_tensor("ident_f", [128, 128], F32))
        ones_bf = E(nc.sbuf_tensor("ones_bf", [128, 1], BF16))
        onesr_f = E(nc.sbuf_tensor("onesr_f", [1, 128], F32))
        dmask = E(nc.sbuf_tensor("dmask", [128, 128], F32))
        e4_sb = E(nc.sbuf_tensor("e4_sb", [4, 64], BF16))
        eps_sb = E(nc.sbuf_tensor("eps_sb", [128, 1], F32))
        rtr_sb = E(nc.sbuf_tensor("rtr_sb", [128, KC * 4], BF16))
        a_sb = E(nc.sbuf_tensor("a_sb", [128, KC * 64], BF16))
        b_sb = E(nc.sbuf_tensor("b_sb", [64, D], BF16))
        st1 = E(nc.sbuf_tensor("st1", [1, 512], F32))
        st2 = E(nc.sbuf_tensor("st2", [1, 512], F32))
        st3 = E(nc.sbuf_tensor("st3", [1, 512], F32))
        st4 = E(nc.sbuf_tensor("st4", [1, 512], F32))
        nmst = E(nc.sbuf_tensor("nmst", [128, 16], F32))
        lst = E(nc.sbuf_tensor("lst", [128, 16], F32))
        alst = E(nc.sbuf_tensor("alst", [128, 16], F32))
        sgm = E(nc.sbuf_tensor("sgm", [128, 4], F32))
        sgl = E(nc.sbuf_tensor("sgl", [128, 4], F32))
        sm1 = E(nc.sbuf_tensor("sm1", [128, 2], F32))
        sm2 = E(nc.sbuf_tensor("sm2", [128, 2], F32))
        lgf = E(nc.sbuf_tensor("lgf", [4, TOK], F32))
        lgt = E(nc.sbuf_tensor("lgt", [128, 32], F32))
        gt1 = E(nc.sbuf_tensor("gt1", [128, 4], F32))
        gt2 = E(nc.sbuf_tensor("gt2", [128, 4], F32))
        gtm = E(nc.sbuf_tensor("gtm", [128, 32], F32))
        gfm = E(nc.sbuf_tensor("gfm", [4, TOK], BF16))
        mid_sb = E(nc.sbuf_tensor("mid_sb", [64, TOK], BF16))
        midw = E(nc.sbuf_tensor("midw", [64, TOK], BF16))
        fm1 = E(nc.sbuf_tensor("fm1", [128, 1], F32))
        fm2 = E(nc.sbuf_tensor("fm2", [128, 1], F32))
        fz = E(nc.sbuf_tensor("fz", [128, 1], F32))

        ps = [E(nc.psum_tensor(f"ps{i}", [128, 512], F32)) for i in range(4)]
        psT_p = [E(nc.psum_tensor(f"psT{i}", [128, 1024], BF16)) for i in range(2)]
        psO_p = [E(nc.psum_tensor(f"psO{i}", [128, 512], F32)) for i in range(2)]
        psT = psT_p[0]
        psO = psO_p[0]

        PGSZ = 512

        def view(start, nel):
            assert start + nel <= 48128, (start, nel)
            return a1[:, start:start + nel], ("a1", start // PGSZ, -(-(start + nel) // PGSZ))

        def pages(v):
            _, p0, p1 = v[1]
            return [("a1", i) for i in range(p0, p1)]

        XN = 0
        xn_v, _ = view(XN, 32768)

        def xn_pages(kcs=None):
            if kcs is None:
                return [("a1", i) for i in range(0, 64)]
            out = []
            for kc_ in kcs:
                out += [("a1", kc_ * 2), ("a1", kc_ * 2 + 1)]
            return out

        WS = 32768
        wslot = [view(WS + i * 4096, 4096)[0] for i in range(3)]

        def w_pages(i):
            b = (WS + i * 4096) // PGSZ
            return [("a1", b + j) for j in range(8)]

        STG = WS + 3 * 4096
        stg = [view(STG + i * 1024, 1024)[0] for i in range(2)]

        def stg_pages(i):
            b = (STG + i * 1024) // PGSZ
            return [("a1", b), ("a1", b + 1)]

        XSQ = STG + 2048
        xsq = [view(XSQ + i * 512, 512)[0] for i in range(2)]

        def xsq_pages(i):
            return [("a1", (XSQ + i * 512) // PGSZ)]

        K0 = 0
        k_sb = [view(K0 + i * 4096, 4096)[0] for i in range(2)]

        def k_pages(i):
            b = (K0 + i * 4096) // PGSZ
            return [("a1", b + j) for j in range(8)]

        V0 = 8192
        v_sb = [view(V0 + i * 4096, 4096)[0] for i in range(2)]

        def v_pages(i):
            b = (V0 + i * 4096) // PGSZ
            return [("a1", b + j) for j in range(8)]

        P0 = 16384
        p_sb, _ = view(P0, 4096)

        def p_pages():
            return [("a1", P0 // PGSZ + j) for j in range(8)]

        PT0 = 20480
        pt_sb, _ = view(PT0, 4096)

        def pt_pages():
            return [("a1", PT0 // PGSZ + j) for j in range(8)]

        AOH = 24576
        aoh_sb, _ = view(AOH, 4096)

        def aoh_pages():
            return [("a1", AOH // PGSZ + j) for j in range(8)]

        VT0 = 28672
        vtmp = [view(VT0 + i * 1024, 1024)[0] for i in range(2)]

        def vt_pages(i):
            b = (VT0 + i * 1024) // PGSZ
            return [("a1", b), ("a1", b + 1)]

        Q0 = 30720
        q_sb = [view(Q0 + i * 512, 512)[0] for i in range(2)]

        def q_pages(i):
            return [("a1", (Q0 + i * 512) // PGSZ)]

        def psp(i):
            return [("ps", i)]

        def dma(eng, out_ap, in_ap, reads, writes, ni=1):
            sem = "gd" if eng == "g" else "yd"

            def fn(e, o=out_ap, i=in_ap):
                return e.dma_start(out=o, in_=i)
            sch.step(eng, fn, reads=reads, writes=writes, sem=sem, inc=16, ni=ni)

        def dma_multi(eng, pairs, reads, writes):
            sem = "gd" if eng == "g" else "yd"

            def fn(e, pairs=pairs):
                return [e.dma_start(out=o, in_=i) for o, i in pairs]
            sch.step(eng, fn, reads=reads, writes=writes, sem=sem,
                     inc=16 * len(pairs), ni=len(pairs))

        dma("g", ident_bf[:, :], ident_in[:, :], [], [("c", "id")])
        dma("g", ident_f[:, :], ident_in[:, :], [], [("c", "idf")])
        dma("g", ones_bf[:, :], ones_in[:, :], [], [("c", "ones")])
        dma("g", onesr_f[:, :], onesr_in[:, :], [], [("c", "onesr")])
        dma("g", dmask[:, :], dmask_in[:, :], [], [("c", "dm")])
        dma("g", e4_sb[:, :], e4_in[:, :], [], [("c", "e4")])
        dma("g", eps_sb[:, :], eps_in[:, :], [], [("c", "eps")])
        dma("g", rtr_sb[:, :], rtr[:, :], [], [("c", "rtr")])
        dma("g", a_sb[:, :], a_pre[:, :], [], [("c", "a")])
        dma("g", b_sb[:, :], b_pre[:, :], [], [("c", "b")])
        for i in range(4):
            dma("g", x_sb[:, i * 8192:(i + 1) * 8192],
                x_in[:, i * 8192:(i + 1) * 8192], [],
                [("x", kc_) for kc_ in range(i * 8, (i + 1) * 8)])

        def emit_ln():
            for ct in range(2):
                cs = ct * 512
                for kc_ in range(KC):
                    par = kc_ % 2

                    def sq_fn(e, kc_=kc_, par=par, cs=cs):
                        return e.activation(xsq[par][:, :], x_sb[:, kc_ * TOK + cs: kc_ * TOK + cs + 512], AF.Square)
                    sch.step("s", sq_fn, reads=[("x", kc_)], writes=xsq_pages(par))

                    def mm_fn(e, kc_=kc_, par=par, cs=cs):
                        e.matmul(ps[2][0:1, :], ones_bf[:, :],
                                 x_sb[:, kc_ * TOK + cs: kc_ * TOK + cs + 512],
                                 start=(kc_ == 0), stop=(kc_ == KC - 1))
                        return e.matmul(ps[3][0:1, :], ones_bf[:, :], xsq[par][:, :],
                                        start=(kc_ == 0), stop=(kc_ == KC - 1))
                    sch.step("t", mm_fn, reads=[("x", kc_), ("c", "ones")] + xsq_pages(par),
                             writes=psp(2) + psp(3), ni=2)
                def stats_v1(e, cs=cs):
                    e.tensor_scalar(st1[:, :], ps[2][0:1, :], 1.0 / D, None, ALU.mult)
                    return e.tensor_scalar(st2[:, :], ps[3][0:1, :], 1.0 / D, None, ALU.mult)
                sch.step("v", stats_v1, reads=psp(2) + psp(3), writes=[("st", 1), ("st", 2)], ni=2)

                def stats_msq(e):
                    return e.tensor_mul(st3[:, :], st1[:, :], st1[:, :])
                sch.step("v", stats_msq, reads=[("st", 1)], writes=[("st", 3)])

                def stats_var(e):
                    return e.tensor_sub(st2[:, :], st2[:, :], st3[:, :])
                sch.step("v", stats_var, reads=[("st", 2), ("st", 3)], writes=[("st", 2)])

                def stats_sqrt(e):
                    return e.activation(st2[:, :], st2[:, :], AF.Sqrt, bias=eps_sb[0:1, 0:1])
                sch.step("s", stats_sqrt, reads=[("st", 2), ("c", "eps")], writes=[("st", 2)])

                def stats_rstd(e):
                    return e.reciprocal(st3[:, :], st2[:, :])
                sch.step("v", stats_rstd, reads=[("st", 2)], writes=[("st", 3)])

                def stats_nmr(e):
                    return e.scalar_tensor_tensor(st4[:, :], st1[:, :], -1.0, st3[:, :],
                                                  ALU.mult, ALU.mult)
                sch.step("v", stats_nmr, reads=[("st", 1), ("st", 3)], writes=[("st", 4)])

                def bcast(e):
                    e.matmul(ps[2][:, :], onesr_f[:, :], st3[:, :], start=True, stop=True)
                    return e.matmul(ps[3][:, :], onesr_f[:, :], st4[:, :], start=True, stop=True)
                sch.step("t", bcast, reads=[("st", 3), ("st", 4), ("c", "onesr")],
                         writes=psp(2) + psp(3), ni=2)

                for kc_ in range(KC):
                    def xn_f1(e, kc_=kc_, cs=cs):
                        dst = xn_v[:, kc_ * TOK + cs: kc_ * TOK + cs + 512]
                        return e.tensor_tensor(dst, x_sb[:, kc_ * TOK + cs: kc_ * TOK + cs + 512],
                                               ps[2][:, :], ALU.mult)
                    sch.step("v", xn_f1, reads=[("x", kc_)] + psp(2),
                             writes=[("a1", kc_ * 2 + ct)])

                    def xn_f2(e, kc_=kc_, cs=cs):
                        dst = xn_v[:, kc_ * TOK + cs: kc_ * TOK + cs + 512]
                        return e.tensor_tensor(dst, dst, ps[3][:, :], ALU.add)
                    sch.step("v", xn_f2, reads=psp(3) + [("a1", kc_ * 2 + ct)],
                             writes=[("a1", kc_ * 2 + ct)])

        def emit_proj(wdram, jmax, dest_fn, scale=None, wslot_off=0):
            for j in range(jmax):
                slot = (j + wslot_off) % 3
                dma("g", wslot[slot][:, :], bass.AP(wdram, j * 128 * D, [[D, 128], [1, D]]),
                    [], w_pages(slot))
                for ct in range(2):
                    pb = (2 * j + ct) % 2
                    cs = ct * 512

                    def mm_fn(e, j=j, slot=slot, pb=pb, cs=cs):
                        last = None
                        for kc_ in range(KC):
                            last = e.matmul(ps[pb][:, :],
                                            wslot[slot][:, kc_ * 128:(kc_ + 1) * 128],
                                            xn_v[:, kc_ * TOK + cs: kc_ * TOK + cs + 512],
                                            start=(kc_ == 0), stop=(kc_ == KC - 1))
                        return last
                    sch.step("t", mm_fn,
                             reads=w_pages(slot) + [("a1", kc_ * 2 + ct) for kc_ in range(KC)],
                             writes=psp(pb), ni=KC)
                    dest_fn(j, ct, pb)

        def stage_to_dram(dram_t, row_off, scale):
            def dest(j, ct, pb, dram_t=dram_t, row_off=row_off, scale=scale):
                spar = j % 2
                cs = ct * 512

                def evac(e, spar=spar, cs=cs, pb=pb):
                    if scale is None:
                        return e.tensor_copy(stg[spar][:, cs:cs + 512], ps[pb][:, :])
                    return e.tensor_scalar(stg[spar][:, cs:cs + 512], ps[pb][:, :],
                                           float(scale), None, ALU.mult)
                sch.step("v", evac, reads=psp(pb), writes=stg_pages(spar))
                if ct == 1:
                    dma("y", bass.AP(dram_t, (row_off + j * 128) * TOK, [[TOK, 128], [1, TOK]]),
                        stg[spar][:, :], stg_pages(spar), [("d", dram_t.name, row_off + j * 128)])
            return dest

        def emit_collective(in_t, out_t2, tag, nrows_in):
            def fn(e, in_t=in_t, out_t2=out_t2):
                return e.collective_compute(
                    "AllGather", ALU.bypass, replica_groups=RG,
                    ins=[in_t.ap().opt()], outs=[out_t2.ap().opt()])
            reads = [("d", in_t.name, r) for r in range(0, nrows_in, 128)]
            sch.step("g", fn, reads=reads, writes=[("d", out_t2.name, 0)], sem="cc", inc=1)

        def emit_collective_rows(in_t, row0, out_t2, tag):
            def fn(e, in_t=in_t, row0=row0, out_t2=out_t2):
                return e.collective_compute(
                    "AllGather", ALU.bypass, replica_groups=RG,
                    ins=[bass.AP(in_t, row0 * TOK, [[TOK, D], [1, TOK]]).opt()],
                    outs=[out_t2.ap().opt()])
            reads = [("d", in_t.name, row0 + r) for r in range(0, D, 128)]
            sch.step("g", fn, reads=reads, writes=[("d", out_t2.name, 0)], sem="cc", inc=1)

        def emit_mirror(src_t, dst_t, parts):
            def fn(e, src_t=src_t, dst_t=dst_t, parts=parts):
                base = sym_off(e, 0, HPC * DK * TOK)
                src = bass.AP(src_t, base,
                              [[parts * D * TOK, RPB], [D * TOK, parts], [1, TOK * TOK]])
                dst = bass.AP(dst_t, 0,
                              [[parts * TOK * TOK, RPB], [TOK * TOK, parts], [1, TOK * TOK]])
                return e.dma_start(out=dst, in_=src)
            sch.step("y", fn, reads=[("d", src_t.name, 0)],
                     writes=[("d", dst_t.name, 0)], sem="yd", inc=16)

        def emit_mirror_part(src_t, part):
            def fn(e, src_t=src_t, part=part):
                base = sym_off(e, 0, HPC * DK * TOK)
                src = bass.AP(src_t, base, [[D * TOK, RPB], [1, TOK * TOK]])
                dst = bass.AP(qkv_m, part * TOK * TOK,
                              [[3 * TOK * TOK, RPB], [1, TOK * TOK]])
                return e.dma_start(out=dst, in_=src)
            sch.step("y", fn, reads=[("d", src_t.name, 0)],
                     writes=[("d", "qkv_m", part)], sem="yd", inc=16)

        def emit_mirror_o(src_t):
            for r4 in range(RPB):
                def fn(e, r4=r4, src_t=src_t):
                    base = sym_off(e, r4 * TOK * SEQ, TOK)
                    src = bass.AP(src_t, base, [[SEQ, TOK], [1, TOK]])
                    dst = bass.AP(o_loc, r4 * TOK * TOK, [[TOK, TOK], [1, TOK]])
                    return e.dma_start(out=dst, in_=src)
                sch.step("y", fn, reads=[("d", src_t.name, 0)],
                         writes=[("d", "o_loc", 0)], sem="yd", inc=16)

        def emit_attention(variant):
            aidx = [0]
            sbctr = [0]

            if variant == "a":
                n_pass = 4
            else:
                WID = max(TOK, DK)
                kpairs, vpairs = [], []
                for r in range(RPB):
                    base = r * (DK + TOK) * WID
                    kpairs.append((k_sb[0][:, r * 1024:(r + 1) * 1024],
                                   bass.AP(kvb_g, base, [[WID, 128], [1, TOK]])))
                    vpairs.append((v_sb[0][:, r * 1024:(r + 1) * 1024],
                                   bass.AP(kvb_g, base + DK * WID,
                                           [[WID, 128], [WID * 128, 8], [1, DK]])))
                dma_multi("y", kpairs, [("d", "kvb_g", 0)], k_pages(0))
                dma_multi("y", vpairs, [("d", "kvb_g", 0)], v_pages(0))
                n_pass = 8

            for pp in range(n_pass):
                if variant == "a":
                    heads = [2 * pp, 2 * pp + 1]
                    for hh, h_ in enumerate(heads):
                        kpairs = []
                        for r in range(RPB):
                            kpairs.append(
                                (k_sb[hh][:, r * 1024:(r + 1) * 1024],
                                 bass.AP(qkv_m, ((r * 3 + 1) * TOK + h_ * DK) * TOK,
                                         [[TOK, 128], [1, TOK]])))

                        def kload(e, kpairs=kpairs):
                            return [e.dma_start(out=o, in_=i) for o, i in kpairs]
                        sch.step("y", kload,
                                 reads=[("d", "qkv_m", 1), ("d", "qkv_m", 0)],
                                 writes=k_pages(hh), sem="yd", inc=16 * RPB, ni=4)
                        for r in range(RPB):
                            vp = r % 2

                            def vload(e, r=r, h_=h_, vp=vp):
                                src = bass.AP(qkv_m, ((r * 3 + 2) * TOK + h_ * DK) * TOK,
                                              [[TOK, 128], [1, TOK]])
                                return e.dma_start(out=vtmp[vp][:, :], in_=src)
                            sch.step("y", vload, reads=[("d", "qkv_m", 2)],
                                     writes=vt_pages(vp), sem="yd", inc=16)

                            def vtr(e, vp=vp):
                                last = None
                                for tt in range(8):
                                    last = e.transpose(psT[:, tt * 128:(tt + 1) * 128],
                                                       vtmp[vp][:, tt * 128:(tt + 1) * 128],
                                                       ident_bf[:, :])
                                return last
                            sch.step("t", vtr, reads=vt_pages(vp) + [("c", "id")],
                                     writes=[("ps", "T", 0)], ni=8)

                            def vev(e, hh=hh, r=r):
                                return e.tensor_copy(
                                    v_sb[hh][:, r * 1024:(r + 1) * 1024], psT[:, :])
                            sch.step("v", vev, reads=[("ps", "T", 0)], writes=v_pages(hh))
                    local_heads = [(0, 0, heads[0]), (1, 1, heads[1])]
                else:
                    local_heads = [(0, 0, pp)]

                for kvp, hh, h_ in local_heads:
                    for qcv in range(8):
                        qpar = qcv % 2

                        def qload(e, h_=h_, qcv=qcv, qpar=qpar, variant=variant):
                            sec = qcv // 2
                            if variant == "a":
                                src = bass.AP(qkv_m,
                                              (sec * 3 * TOK + h_ * DK) * TOK + (qcv % 2) * 512,
                                              [[TOK, 128], [1, 512]])
                            else:
                                src = bass.AP(qb_m,
                                              (sec * TOK + h_ * DK) * TOK + (qcv % 2) * 512,
                                              [[TOK, 128], [1, 512]])
                            return e.dma_start(out=q_sb[qpar][:, :], in_=src)
                        sch.step("y", qload,
                                 reads=[("d", "qkv_m" if variant == "a" else "qb_m", 0)],
                                 writes=q_pages(qpar), sem="yd", inc=16)

                        for qt in range(4):
                            g = qcv * 4 + qt
                            nb = g + 1
                            span = nb * 128
                            ngrp = (span + 511) // 512
                            aidx[0] += 1
                            par = aidx[0] % 2
                            pB = 16384 if par == 0 else 32768
                            ptB = 20480 if par == 0 else 36864
                            pPg = pB // PGSZ
                            ptPg = ptB // PGSZ
                            nmv = nmst[:, par * 8:par * 8 + 8]
                            lsv = lst[:, par * 8:par * 8 + 8]
                            alv = alst[:, par * 8:par * 8 + 8]
                            sgmv = sgm[:, par * 2:par * 2 + 2]
                            sglv = sgl[:, par * 2:par * 2 + 2]
                            sm1v = sm1[:, par:par + 1]
                            sm2v = sm2[:, par:par + 1]
                            K_NM, K_L, K_AL = ("st", "nm", par), ("st", "l", par), ("st", "al", par)
                            K_SGM, K_SGL = ("st", "sgm", par), ("st", "sgl", par)
                            K_GM, K_LS = ("st", "gm", par), ("st", "lsum", par)
                            nsg = (ngrp + 3) // 4
                            assert nsg <= 2
                            banks = {}
                            for sg in range(nsg):
                                glo, ghi = sg * 4, min(sg * 4 + 4, ngrp)
                                for gi in range(glo, ghi):
                                    w_ = min(512, span - gi * 512)
                                    bank = sbctr[0] % 4
                                    sbctr[0] += 1
                                    banks[gi] = bank

                                    def smm(e, gi=gi, w_=w_, bank=bank, qt=qt, kvp=kvp,
                                            qpar=qpar):
                                        return e.matmul(
                                            ps[bank][:, 0:w_],
                                            q_sb[qpar][:, qt * 128:(qt + 1) * 128],
                                            k_sb[kvp][:, gi * 512: gi * 512 + w_],
                                            start=True, stop=True)
                                    sch.step("t", smm, reads=q_pages(qpar) + k_pages(kvp),
                                             writes=psp(bank))
                                    if gi == ngrp - 1:
                                        dcol = (nb - 1) * 128 - gi * 512

                                        def dmsk(e, bank=bank, dcol=dcol):
                                            return e.tensor_tensor(ps[bank][:, dcol:dcol + 128],
                                                                   ps[bank][:, dcol:dcol + 128],
                                                                   dmask[:, :], ALU.add)
                                        sch.step("v", dmsk, reads=psp(bank) + [("c", "dm")],
                                                 writes=psp(bank))

                                    def rmax(e, gi=gi, w_=w_, bank=bank, nmv=nmv):
                                        return e.reduce_max(nmv[:, gi:gi + 1], ps[bank][:, 0:w_],
                                                            axis=AX.X, negate=True)
                                    sch.step("v", rmax, reads=psp(bank), writes=[K_NM])
                                if ghi - glo > 1:
                                    def sgmin(e, glo=glo, ghi=ghi, sg=sg, nmv=nmv, sgmv=sgmv):
                                        return e.tensor_reduce(sgmv[:, sg:sg + 1],
                                                               nmv[:, glo:ghi], AX.X, ALU.min)
                                    sch.step("v", sgmin, reads=[K_NM], writes=[K_SGM])
                                else:
                                    def sgcp(e, glo=glo, sg=sg, nmv=nmv, sgmv=sgmv):
                                        return e.tensor_copy(sgmv[:, sg:sg + 1], nmv[:, glo:glo + 1])
                                    sch.step("v", sgcp, reads=[K_NM], writes=[K_SGM])
                                for gi in range(glo, ghi):
                                    w_ = min(512, span - gi * 512)
                                    bank = banks[gi]

                                    def expf(e, gi=gi, w_=w_, bank=bank, sg=sg, pB=pB,
                                             sgmv=sgmv, lsv=lsv):
                                        return e.activation(a1[:, pB + gi * 512: pB + gi * 512 + w_],
                                                            ps[bank][:, 0:w_], AF.Exp,
                                                            bias=sgmv[:, sg:sg + 1],
                                                            accum_out=lsv[:, gi:gi + 1])
                                    sch.step("s", expf, reads=psp(bank) + [K_SGM],
                                             writes=[("a1", pPg + gi), K_L])
                            for sg in range(nsg):
                                glo, ghi = sg * 4, min(sg * 4 + 4, ngrp)
                                if ghi - glo > 1:
                                    def lsumf(e, glo=glo, ghi=ghi, sg=sg, lsv=lsv, sglv=sglv):
                                        return e.reduce_sum(sglv[:, sg:sg + 1], lsv[:, glo:ghi],
                                                            axis=AX.X)
                                    sch.step("v", lsumf, reads=[K_L], writes=[K_SGL])
                                else:
                                    def lcp(e, glo=glo, sg=sg, lsv=lsv, sglv=sglv):
                                        return e.tensor_copy(sglv[:, sg:sg + 1], lsv[:, glo:glo + 1])
                                    sch.step("v", lcp, reads=[K_L], writes=[K_SGL])
                            if nsg > 1:
                                def gmin(e, nsg=nsg, sgmv=sgmv, sm1v=sm1v):
                                    return e.tensor_reduce(sm1v, sgmv[:, 0:nsg], AX.X, ALU.min)
                                sch.step("v", gmin, reads=[K_SGM], writes=[K_GM])

                                def aarg(e, nsg=nsg, sgmv=sgmv, sm1v=sm1v, alv=alv):
                                    return e.tensor_scalar(alv[:, 0:nsg], sgmv[:, 0:nsg],
                                                           -1.0, sm1v, ALU.mult, ALU.add)
                                sch.step("v", aarg, reads=[K_SGM, K_GM], writes=[K_AL])

                                def aexp(e, nsg=nsg, alv=alv):
                                    return e.activation(alv[:, 0:nsg], alv[:, 0:nsg], AF.Exp)
                                sch.step("s", aexp, reads=[K_AL], writes=[K_AL])

                                def lw_f(e, nsg=nsg, alv=alv, sglv=sglv):
                                    return e.tensor_mul(sglv[:, 0:nsg], alv[:, 0:nsg], sglv[:, 0:nsg])
                                sch.step("v", lw_f, reads=[K_AL, K_SGL], writes=[K_SGL])

                                def lsum2(e, nsg=nsg, sglv=sglv, sm2v=sm2v):
                                    return e.reduce_sum(sm2v, sglv[:, 0:nsg], axis=AX.X)
                                sch.step("v", lsum2, reads=[K_SGL], writes=[K_LS])

                                def lrec(e, sm2v=sm2v):
                                    return e.reciprocal(sm2v, sm2v)
                                sch.step("v", lrec, reads=[K_LS], writes=[K_LS])

                                def coef_f(e, nsg=nsg, alv=alv, sm2v=sm2v):
                                    return e.tensor_scalar(alv[:, 0:nsg], alv[:, 0:nsg],
                                                           sm2v, None, ALU.mult)
                                sch.step("v", coef_f, reads=[K_AL, K_LS], writes=[K_AL])
                                for gi in range(ngrp):
                                    w_ = min(512, span - gi * 512)
                                    sg = gi // 4

                                    def pnorm(e, gi=gi, w_=w_, sg=sg, pB=pB, alv=alv):
                                        return e.tensor_scalar(a1[:, pB + gi * 512: pB + gi * 512 + w_],
                                                               a1[:, pB + gi * 512: pB + gi * 512 + w_],
                                                               alv[:, sg:sg + 1], None, ALU.mult)
                                    sch.step("v", pnorm,
                                             reads=[("a1", pPg + gi), K_AL],
                                             writes=[("a1", pPg + gi)])
                            else:
                                def rle(e, sglv=sglv, sm2v=sm2v):
                                    return e.reciprocal(sm2v, sglv[:, 0:1])
                                sch.step("v", rle, reads=[K_SGL], writes=[K_LS])
                                for gi in range(ngrp):
                                    w_ = min(512, span - gi * 512)

                                    def pnorm(e, gi=gi, w_=w_, pB=pB, sm2v=sm2v):
                                        return e.tensor_scalar(a1[:, pB + gi * 512: pB + gi * 512 + w_],
                                                               a1[:, pB + gi * 512: pB + gi * 512 + w_],
                                                               sm2v, None, ALU.mult)
                                    sch.step("v", pnorm,
                                             reads=[("a1", pPg + gi), K_LS],
                                             writes=[("a1", pPg + gi)])
                            for kb0 in range(0, nb, 8):
                                kbn = min(8, nb - kb0)

                                def trf(e, kb0=kb0, kbn=kbn, pB=pB, par=par):
                                    last = None
                                    for k_ in range(kbn):
                                        last = e.transpose(psT_p[par][:, k_ * 128:(k_ + 1) * 128],
                                                           a1[:, pB + (kb0 + k_) * 128: pB + (kb0 + k_ + 1) * 128],
                                                           ident_bf[:, :])
                                    return last
                                sch.step("t", trf,
                                         reads=[("a1", pPg + (kb0 // 4) + j) for j in range(2)] + [("c", "id")],
                                         writes=[("ps", "T", par)], ni=kbn)

                                def tev(e, kb0=kb0, kbn=kbn, ptB=ptB, par=par):
                                    return e.tensor_copy(a1[:, ptB + kb0 * 128: ptB + (kb0 + kbn) * 128],
                                                         psT_p[par][:, 0:kbn * 128])
                                sch.step("v", tev, reads=[("ps", "T", par)],
                                         writes=[("a1", ptPg + kb0 // 4),
                                                 ("a1", ptPg + kb0 // 4 + 1)])

                            def pv(e, nb=nb, kvp=kvp, qt=qt, ptB=ptB, par=par):
                                last = None
                                for kb in range(nb):
                                    last = e.matmul(psO_p[par][:, (qt % 4) * 128:(qt % 4) * 128 + 128],
                                                    v_sb[kvp][:, kb * 128: (kb + 1) * 128],
                                                    a1[:, ptB + kb * 128: ptB + (kb + 1) * 128],
                                                    start=(kb == 0), stop=(kb == nb - 1))
                                return last
                            sch.step("t", pv,
                                     reads=[("a1", ptPg + j) for j in range(8)] + v_pages(kvp),
                                     writes=[("ps", "O", par)], ni=nb)

                            def oev(e, qcv=qcv, qt=qt, par=par):
                                return e.tensor_copy(aoh_sb[:, qcv * 512 + qt * 128: qcv * 512 + (qt + 1) * 128],
                                                     psO_p[par][:, (qt % 4) * 128:(qt % 4) * 128 + 128])
                            sch.step("v", oev, reads=[("ps", "O", par)], writes=aoh_pages())
                    myh = h_
                    opairs = []
                    for r in range(RPB):
                        opairs.append((bass.AP(o_in, (myh * DK) * SEQ + r * 1024, [[SEQ, 128], [1, 1024]]),
                                       aoh_sb[:, r * 1024:(r + 1) * 1024]))
                    dma_multi("y", opairs, aoh_pages(), [("d", "o_in", myh * 128)])

        def emit_outproj(wdram, gathered, wslot_off=0):
            for kc_ in range(KC):
                def afm_load(e, kc_=kc_):
                    src = bass.AP(o_loc, (kc_ * 128) * TOK, [[TOK, 128], [1, TOK]])
                    return e.dma_start(out=xn_v[:, kc_ * TOK:(kc_ + 1) * TOK], in_=src)
                sch.step("y", afm_load, reads=[("d", "o_loc", 0)],
                         writes=[("a1", kc_ * 2), ("a1", kc_ * 2 + 1)], sem="yd", inc=16)
            for j in range(KC):
                slot = (j + wslot_off) % 3
                dma("g", wslot[slot][:, :], bass.AP(wdram, j * 128 * D, [[D, 128], [1, D]]),
                    [], w_pages(slot))
                for ct in range(2):
                    pb = (2 * j + ct) % 2
                    cs = ct * 512

                    def mm_fn(e, j=j, slot=slot, pb=pb, cs=cs):
                        last = None
                        for kc_ in range(KC):
                            last = e.matmul(ps[pb][:, :],
                                            wslot[slot][:, kc_ * 128:(kc_ + 1) * 128],
                                            xn_v[:, kc_ * TOK + cs: kc_ * TOK + cs + 512],
                                            start=(kc_ == 0), stop=(kc_ == KC - 1))
                        return last
                    sch.step("t", mm_fn,
                             reads=w_pages(slot) + [("a1", kc_ * 2 + ct) for kc_ in range(KC)],
                             writes=psp(pb), ni=KC)

                    def radd(e, j=j, pb=pb, cs=cs):
                        dst = x_sb[:, j * TOK + cs: j * TOK + cs + 512]
                        return e.tensor_tensor(dst, dst, ps[pb][:, :], ALU.add)
                    sch.step("v", radd, reads=psp(pb) + [("x", j)], writes=[("x", j)])

        def emit_ffn():
            for ct in range(2):
                cs = ct * 512

                def lmm(e, cs=cs):
                    last = None
                    for kc_ in range(KC):
                        last = e.matmul(ps[2][0:4, :], rtr_sb[:, kc_ * 4:(kc_ + 1) * 4],
                                        xn_v[:, kc_ * TOK + cs: kc_ * TOK + cs + 512],
                                        start=(kc_ == 0), stop=(kc_ == KC - 1))
                    return last
                sch.step("t", lmm, reads=[("c", "rtr")] + [("a1", kc_ * 2 + ct) for kc_ in range(KC)],
                         writes=psp(2), ni=KC)

                def lev(e, cs=cs):
                    return e.tensor_copy(lgf[:, cs:cs + 512], ps[2][0:4, :])
                sch.step("v", lev, reads=psp(2), writes=[("st", "lgf")])
            def ltr(e):
                last = None
                for tt in range(8):
                    last = e.transpose(psO[:, tt * 4:(tt + 1) * 4],
                                       lgf[0:4, tt * 128:(tt + 1) * 128],
                                       ident_f[0:4, 0:4])
                return last
            sch.step("t", ltr, reads=[("st", "lgf"), ("c", "idf")], writes=[("ps", "O", 0)], ni=8)

            def ltv(e):
                return e.tensor_copy(lgt[:, 0:32], psO[:, 0:32])
            sch.step("v", ltv, reads=[("ps", "O", 0)], writes=[("st", "lgt")])

            for tt in range(8):
                sl = lgt[:, tt * 4:(tt + 1) * 4]

                def g1(e, sl=sl):
                    e.reduce_max(fm1[:, :], sl, axis=AX.X)
                    return e.reduce_max(fm2[:, :], sl, axis=AX.X, negate=True)
                sch.step("v", g1, reads=[("st", "lgt")], writes=[("st", "fm1"), ("st", "fm2")], ni=2)

                def g2(e, sl=sl):
                    return e.tensor_scalar(gt1[:, :], sl, fm1[:, 0:1], None, ALU.is_equal)
                sch.step("v", g2, reads=[("st", "lgt"), ("st", "fm1")], writes=[("st", "gt1")])

                def g3(e, sl=sl):
                    return e.scalar_tensor_tensor(gt2[:, :], gt1[:, :], MASKV, sl, ALU.mult, ALU.add)
                sch.step("v", g3, reads=[("st", "gt1"), ("st", "lgt")], writes=[("st", "gt2")])

                def g4(e):
                    return e.reduce_max(fz[:, :], gt2[:, :], axis=AX.X)
                sch.step("v", g4, reads=[("st", "gt2")], writes=[("st", "fz")])

                def g5(e, sl=sl):
                    return e.tensor_scalar(gt1[:, :], sl, fz[:, 0:1], None, ALU.is_ge)
                sch.step("v", g5, reads=[("st", "lgt"), ("st", "fz")], writes=[("st", "gt1")])

                def g6(e, sl=sl):
                    return e.activation(gt2[:, :], sl, AF.Exp, bias=fm2[:, 0:1])
                sch.step("s", g6, reads=[("st", "lgt"), ("st", "fm2")], writes=[("st", "gt2")])

                def g7a(e):
                    return e.tensor_mul(gt2[:, :], gt2[:, :], gt1[:, :])
                sch.step("v", g7a, reads=[("st", "gt1"), ("st", "gt2")], writes=[("st", "gt2")])

                def g7b(e):
                    return e.reduce_sum(fz[:, :], gt2[:, :], axis=AX.X)
                sch.step("v", g7b, reads=[("st", "gt2")], writes=[("st", "fz")])

                def g8a(e):
                    return e.reciprocal(fz[:, :], fz[:, :])
                sch.step("v", g8a, reads=[("st", "fz")], writes=[("st", "fz")])

                def g8b(e, tt=tt):
                    return e.tensor_scalar(gtm[:, tt * 4:(tt + 1) * 4], gt2[:, :],
                                           fz[:, 0:1], None, ALU.mult)
                sch.step("v", g8b, reads=[("st", "gt2"), ("st", "fz")], writes=[("st", "gtm")])

            for half in range(2):
                def gtr(e, half=half):
                    last = None
                    for tt in range(4):
                        last = e.transpose(psO[0:4, tt * 128:(tt + 1) * 128],
                                           gtm[:, (half * 4 + tt) * 4:(half * 4 + tt + 1) * 4],
                                           ident_f[:, :])
                    return last
                sch.step("t", gtr, reads=[("st", "gtm"), ("c", "idf")],
                         writes=[("ps", "O", 0)], ni=4)

                def gtv(e, half=half):
                    return e.tensor_copy(gfm[:, half * 512:(half + 1) * 512], psO[0:4, 0:512])
                sch.step("v", gtv,
                         reads=[("ps", "O", 0)],
                         writes=[("st", "gfm")])

            for ct in range(2):
                cs = ct * 512

                def midmm(e, cs=cs):
                    last = None
                    for kc_ in range(KC):
                        last = e.matmul(ps[2][0:64, :], a_sb[:, kc_ * 64:(kc_ + 1) * 64],
                                        xn_v[:, kc_ * TOK + cs: kc_ * TOK + cs + 512],
                                        start=(kc_ == 0), stop=(kc_ == KC - 1))
                    return last
                sch.step("t", midmm, reads=[("c", "a")] + [("a1", kc_ * 2 + ct) for kc_ in range(KC)],
                         writes=psp(2), ni=KC)

                def midev(e, cs=cs):
                    return e.tensor_copy(mid_sb[:, cs:cs + 512], ps[2][0:64, :])
                sch.step("v", midev, reads=psp(2), writes=[("st", "mid")])

                def g64mm(e, cs=cs):
                    return e.matmul(ps[3][0:64, :], e4_sb[:, :], gfm[:, cs:cs + 512],
                                    start=True, stop=True)
                sch.step("t", g64mm, reads=[("c", "e4"), ("st", "gfm")], writes=psp(3))

                def midw_f(e, cs=cs):
                    return e.tensor_tensor(midw[:, cs:cs + 512], mid_sb[:, cs:cs + 512],
                                           ps[3][0:64, :], ALU.mult)
                sch.step("v", midw_f, reads=[("st", "mid")] + psp(3), writes=[("st", "midw")])

            for j in range(KC):
                for ct in range(2):
                    pb = (2 * j + ct) % 2
                    cs = ct * 512

                    def dmm(e, j=j, pb=pb, cs=cs):
                        return e.matmul(ps[pb][:, :], b_sb[:, j * 128:(j + 1) * 128],
                                        midw[:, cs:cs + 512], start=True, stop=True)
                    sch.step("t", dmm, reads=[("c", "b"), ("st", "midw")], writes=psp(pb))

                    def radd(e, j=j, pb=pb, cs=cs):
                        dst = x_sb[:, j * TOK + cs: j * TOK + cs + 512]
                        return e.tensor_tensor(dst, dst, ps[pb][:, :], ALU.add)
                    sch.step("v", radd, reads=psp(pb) + [("x", j)], writes=[("x", j)])

        if stage == 0:
            emit_ln()
            for i in range(4):
                def xdmp(e, i=i):
                    return e.dma_start(out=out_t[:, i * 8192:(i + 1) * 8192],
                                       in_=xn_v[:, i * 8192:(i + 1) * 8192])
                sch.step("g", xdmp, reads=xn_pages(), writes=[("o", i)], sem="gd", inc=16)
            def ddmp0(e):
                return e.dma_start(out=dbg_t[0:128, :], in_=x_sb[:, 0:1024])
            sch.step("g", ddmp0, reads=[("x", 0)], writes=[("dbg", 0)], sem="gd", inc=16)
            sch.final_wait("g", "gd")
            with nc.Block() as block:
                @block.tensor
                def _(t):
                    sch.replay("t", t, sems)

                @block.vector
                def _(v):
                    sch.replay("v", v, sems)

                @block.scalar
                def _(s):
                    sch.replay("s", s, sems)

                @block.gpsimd
                def _(g):
                    sch.replay("g", g, sems)

                @block.sync
                def _(y):
                    sch.replay("y", y, sems)
            print(f"[kernel] built dbg stage {stage}")
            return nc

        emit_ln()
        if stage == 0.5:
            emit_proj(wk_a, KC, stage_to_dram(qkv_in, D, None))
            emit_proj(wv_a, KC, stage_to_dram(qkv_in, 2 * D, None), wslot_off=2)
            emit_proj(wq_a, KC, stage_to_dram(qkv_in, 0, SCALE_A), wslot_off=1)
            emit_collective(qkv_in, qkv_g, "qkv_a", 3 * D)
            emit_mirror(qkv_g, qkv_m, 3)
            for i in range(12):
                def ddmp(e, i=i):
                    return e.dma_start(out=dbg_t[i * TOK:(i + 1) * TOK, :],
                                       in_=bass.AP(qkv_m, i * TOK * TOK, [[TOK, TOK], [1, TOK]]))
                sch.step("g", ddmp, reads=[("d", "qkv_m", 0)], writes=[("dbg", i)], sem="gd", inc=16)
            for i in range(4):
                def xdmp(e, i=i):
                    return e.dma_start(out=out_t[:, i * 8192:(i + 1) * 8192],
                                       in_=x_sb[:, i * 8192:(i + 1) * 8192])
                sch.step("g", xdmp, reads=[("x", k) for k in range(i * 8, (i + 1) * 8)],
                         writes=[("o", i)], sem="gd", inc=16)
            sch.final_wait("g", "gd")
            with nc.Block() as block:
                @block.tensor
                def _(t):
                    sch.replay("t", t, sems)

                @block.vector
                def _(v):
                    sch.replay("v", v, sems)

                @block.scalar
                def _(s):
                    sch.replay("s", s, sems)

                @block.gpsimd
                def _(g):
                    sch.replay("g", g, sems)

                @block.sync
                def _(y):
                    sch.replay("y", y, sems)
            print(f"[kernel] built dbg stage {stage}")
            return nc

        emit_proj(wk_a, KC, stage_to_dram(qkv_in, D, None))
        emit_proj(wv_a, KC, stage_to_dram(qkv_in, 2 * D, None), wslot_off=2)
        emit_proj(wq_a, KC, stage_to_dram(qkv_in, 0, SCALE_A), wslot_off=1)
        emit_collective(qkv_in, qkv_g, "qkv_a", 3 * D)
        emit_mirror(qkv_g, qkv_m, 3)
        emit_attention("a")
        if stage == 0.8:
            for i in range(8):
                def odmp(e, i=i):
                    return e.dma_start(out=dbg_t[i * 128:(i + 1) * 128, :],
                                       in_=bass.AP(o_in, i * 128 * SEQ, [[SEQ, 128], [1, SEQ]]))
                sch.step("g", odmp, reads=[("d", "o_in", i * 128)], writes=[("dbg", i)],
                         sem="gd", inc=16)
        emit_collective(o_in, o_g, "o_a", HPC * DK)
        emit_mirror_o(o_g)
        emit_outproj(wo_a, o_g)

        if stage >= 2 or stage == 1.5:
            emit_ln()
            emit_ffn()
            if stage == 1.5:
                def dmp1(e):
                    return e.dma_start(out=dbg_t[0:128, 0:32], in_=lgt[:, :])
                sch.step("g", dmp1, reads=[("st", "lgt")], writes=[("dbg", 1)], sem="gd", inc=16)

                def dmp2(e):
                    return e.dma_start(out=dbg_t[0:128, 32:64], in_=gtm[:, :])
                sch.step("g", dmp2, reads=[("st", "gtm")], writes=[("dbg", 2)], sem="gd", inc=16)

                def dmp3(e):
                    return e.dma_start(out=dbg_t[128:132, :], in_=gfm[:, :])
                sch.step("g", dmp3, reads=[("st", "gfm")], writes=[("dbg", 3)], sem="gd", inc=16)

                def dmp4(e):
                    return e.dma_start(out=dbg_t[132:196, :], in_=mid_sb[:, :])
                sch.step("g", dmp4, reads=[("st", "mid")], writes=[("dbg", 4)], sem="gd", inc=16)

                def dmp5(e):
                    return e.dma_start(out=dbg_t[196:260, :], in_=midw[:, :])
                sch.step("g", dmp5, reads=[("st", "midw")], writes=[("dbg", 5)], sem="gd", inc=16)

        if stage >= 3:
            emit_ln()
            def kb_dest(j, ct, pb):
                spar = 0
                cs = ct * 512

                def evac(e, cs=cs, pb=pb):
                    return e.tensor_copy(stg[spar][:, cs:cs + 512], ps[pb][:, :])
                sch.step("v", evac, reads=psp(pb), writes=stg_pages(spar))
                if ct == 1:
                    WID = max(TOK, DK)
                    dma("y", bass.AP(kvb_in, 0, [[WID, 128], [1, TOK]]), stg[spar][:, :],
                        stg_pages(spar), [("d", "kvb_in", 0)])
            emit_proj(wk_b, 1, kb_dest)

            def vb_dest(j, ct, pb):
                spar = 1
                cs = ct * 512

                def evac(e, cs=cs, pb=pb):
                    return e.tensor_copy(stg[spar][:, cs:cs + 512], ps[pb][:, :])
                sch.step("v", evac, reads=psp(pb), writes=stg_pages(spar))
                if ct == 1:
                    def vtr(e):
                        last = None
                        for t8 in range(8):
                            last = e.transpose(psT[:, t8 * 128:(t8 + 1) * 128],
                                               stg[spar][:, t8 * 128:(t8 + 1) * 128],
                                               ident_bf[:, :])
                        return last
                    sch.step("t", vtr, reads=stg_pages(spar) + [("c", "id")],
                             writes=[("ps", "T", 0)], ni=8)

                    def vev(e):
                        return e.tensor_copy(stg[0][:, :], psT[:, :])
                    sch.step("v", vev, reads=[("ps", "T", 0)], writes=stg_pages(0))
                    WID = max(TOK, DK)
                    pairs = []
                    for t8 in range(8):
                        pairs.append((bass.AP(kvb_in, (DK + t8 * 128) * WID, [[WID, 128], [1, DK]]),
                                      stg[0][:, t8 * 128:(t8 + 1) * 128]))
                    dma_multi("y", pairs, stg_pages(0),
                              [("d", "kvb_in", r) for r in range(128, DK + TOK, 128)])
            emit_proj(wv_b, 1, vb_dest, wslot_off=1)
            emit_proj(wq_b, KC, stage_to_dram(q_in_b, 0, SCALE_B), wslot_off=2)
            emit_collective(kvb_in, kvb_g, "kv_b", DK + TOK)
            emit_collective(q_in_b, q_g_b, "q_b", D)
            emit_mirror(q_g_b, qb_m, 1)
            emit_attention("b")
            emit_collective(o_in, o_g, "o_b", HPC * DK)
            emit_mirror_o(o_g)
            emit_outproj(wo_b, o_g)

        if stage >= 4:
            emit_ln()
            emit_ffn()

        for i in range(4):
            dma("g", out_t[:, i * 8192:(i + 1) * 8192], x_sb[:, i * 8192:(i + 1) * 8192],
                [("x", kc_) for kc_ in range(i * 8, (i + 1) * 8)], [("o", i)])
        sch.final_wait("g", "gd")

        with nc.Block() as block:
            @block.tensor
            def _(t):
                sch.replay("t", t, sems)

            @block.vector
            def _(v):
                sch.replay("v", v, sems)

            @block.scalar
            def _(s):
                sch.replay("s", s, sems)

            @block.gpsimd
            def _(g):
                sch.replay("g", g, sems)

            @block.sync
            def _(y):
                sch.replay("y", y, sems)

    print(f"[kernel] built: {sch.n_instr} compute/dma instrs, "
          f"steps t={len(sch.steps['t'])} v={len(sch.steps['v'])} "
          f"s={len(sch.steps['s'])} g={len(sch.steps['g'])} y={len(sch.steps['y'])}")
    return nc



def _prep_w(w, cols=None):
    if cols is not None:
        w = w[:, cols]
    din, dout = w.shape
    nj = dout // 128
    nk = din // 128
    pre = np.ascontiguousarray(
        w.reshape(nk, 128, nj, 128).transpose(2, 1, 0, 3).reshape(nj, 128, din)
        .astype(np.float32))
    return pre


def kernel(L0_attn_a_hidden_states, L0_attn_a_wq, L0_attn_a_wkv, L0_attn_a_wo,
           L0_attn_b_wq, L0_attn_b_wk, L0_attn_b_wv, L0_attn_b_wo,
           L0_ffn_router_w, L0_ffn_A, L0_ffn_B):
    x = np.asarray(L0_attn_a_hidden_states, np.float32)
    H_, DQ, DN, DV = 32, 192, 128, 128

    qcols = np.concatenate([np.arange(h * DQ, h * DQ + DN) for h in range(H_)])
    wq_a = _prep_w(np.asarray(L0_attn_a_wq, np.float32), qcols)
    wkv = np.asarray(L0_attn_a_wkv, np.float32)
    wk_a = _prep_w(wkv[:, :H_ * DN])
    wv_a = _prep_w(wkv[:, H_ * DN:H_ * (DN + DV)])
    wo_a = _prep_w(np.asarray(L0_attn_a_wo, np.float32))
    wq_b = _prep_w(np.asarray(L0_attn_b_wq, np.float32))
    wk_b = _prep_w(np.asarray(L0_attn_b_wk, np.float32))
    wv_b = _prep_w(np.asarray(L0_attn_b_wv, np.float32))
    wo_b = _prep_w(np.asarray(L0_attn_b_wo, np.float32))

    rw = np.asarray(L0_ffn_router_w, np.float32)
    rtr = np.ascontiguousarray(rw.T.reshape(KC, 128, 4).transpose(1, 0, 2).reshape(128, KC * 4))
    A = np.asarray(L0_ffn_A, np.float32).reshape(64, D)
    a_pre = np.ascontiguousarray(A.T.reshape(KC, 128, 64).transpose(1, 0, 2).reshape(128, KC * 64))
    Bw = np.asarray(L0_ffn_B, np.float32)
    b_pre = np.ascontiguousarray(Bw.transpose(0, 2, 1).reshape(64, D))

    ident = np.eye(128, dtype=np.float32)
    ones_c = np.ones((128, 1), np.float32)
    ones_r = np.ones((1, 128), np.float32)
    dmaskv = np.triu(np.full((128, 128), np.float32(MASKV), np.float32), 1)
    eps_c = np.full((128, 1), 1e-5, np.float32)
    e4 = np.zeros((4, 64), np.float32)
    for e_ in range(4):
        e4[e_, e_ * 16:(e_ + 1) * 16] = LORA_SCALE

    shared = dict(wq_a=wq_a, wk_a=wk_a, wv_a=wv_a, wo_a=wo_a, wq_b=wq_b,
                  wk_b=wk_b, wv_b=wv_b, wo_b=wo_b, rtr=rtr, a_pre=a_pre,
                  b_pre=b_pre, ident_in=ident, ones_in=ones_c, onesr_in=ones_r,
                  dmask_in=dmaskv, e4_in=e4, eps_in=eps_c)

    in_maps = []
    for c in range(NCORES):
        b, r = c // RPB, c % RPB
        xs = x[b, r * TOK:(r + 1) * TOK, :]
        xfm = np.ascontiguousarray(
            xs.T.reshape(KC, 128, TOK).transpose(1, 0, 2).reshape(128, KC * TOK))
        in_maps.append({**shared, "x_in": xfm})

    nc = build_nc(STAGE)
    res = run_bass_kernel_spmd(nc, in_maps, core_ids=list(range(NCORES)))

    out = np.empty((BATCH, SEQ, D), np.float32)
    for c in range(NCORES):
        b, r = c // RPB, c % RPB
        o = res.results[c]["out"]
        out[b, r * TOK:(r + 1) * TOK, :] = (
            o.reshape(128, KC, TOK).transpose(2, 1, 0).reshape(TOK, D))
    if STAGE in (0, 0.5, 0.8, 1.5):
        kernel.dbg = [res.results[c].get("dbg") for c in range(NCORES)]
    return out


# version 06
# speedup vs baseline: 1.0219x; 1.0219x over previous
import math
import contextlib
import traceback as _tb
import numpy as np

_orig_ffs = _tb.StackSummary.format_frame_summary

def _safe_ffs(self, frame_summary, **kw):
    try:
        return _orig_ffs(self, frame_summary, **kw)
    except Exception:
        return f"  File {frame_summary.filename}, line {frame_summary.lineno}\n"

_tb.StackSummary.format_frame_summary = _safe_ffs
import concourse.bass as bass
import concourse.mybir as mybir
from concourse.bass_utils import run_bass_kernel_spmd

F32 = mybir.dt.float32
BF16 = mybir.dt.bfloat16
AF = mybir.ActivationFunctionType
ALU = mybir.AluOpType
AX = mybir.AxisListType

D = 4096
BATCH = 2
SEQ = 4096
NCORES = 8
RPB = 4
TOK = 1024
KC = 32
H = 32
HPC = 8
DK = 128
NQT = 32
SCALE_A = float(192.0 ** -0.5)
SCALE_B = float(128.0 ** -0.5)
MASKV = -1000.0
LORA_SCALE = 4.0

STAGE = 4
SEQ_MODE = False
BARRIER_ENG = ""


ENG_SEM = {"t": "t", "v": "v", "s": "s", "g": "gc", "y": "yc"}


class Sched:

    NROT = {"yd": 16, "gd": 8, "cc": 4}

    def __init__(self):
        self.steps = {"t": [], "v": [], "s": [], "g": [], "y": []}
        self.cnt = {}
        self.pages = {}
        self.n_instr = 0
        self.rot = {"yd": 0, "gd": 0, "cc": 0}

    def _touch(self, key):
        if key not in self.pages:
            self.pages[key] = {"w": {}, "r": {}}
        return self.pages[key]

    def step(self, eng, fn, reads=(), writes=(), sem=None, inc=1, ni=1):
        if sem is None:
            sem = ENG_SEM[eng]
        elif sem in self.rot:
            n = self.rot[sem]
            self.rot[sem] += 1
            sem = f"{sem}{n % self.NROT[sem]}"
        waits = {}
        if SEQ_MODE or eng in BARRIER_ENG:
            for s_, c_ in self.cnt.items():
                if c_ > 0:
                    waits[s_] = c_
        own = ENG_SEM[eng] if eng == "t" else None
        for k in reads:
            pg = self._touch(k)
            for s_, c_ in pg["w"].items():
                if s_ != own:
                    waits[s_] = max(waits.get(s_, 0), c_)
        for k in writes:
            pg = self._touch(k)
            for s_, c_ in pg["w"].items():
                if s_ != own:
                    waits[s_] = max(waits.get(s_, 0), c_)
            for s_, c_ in pg["r"].items():
                if s_ != own:
                    waits[s_] = max(waits.get(s_, 0), c_)
        new_cnt = self.cnt.get(sem, 0) + inc
        self.cnt[sem] = new_cnt
        for k in reads:
            pg = self._touch(k)
            pg["r"][sem] = new_cnt
        for k in writes:
            pg = self._touch(k)
            pg["w"] = {sem: new_cnt}
            pg["r"] = {}
        self.steps[eng].append((dict(waits), fn, sem, inc))
        self.n_instr += ni

    def replay(self, eng_name, eng, sems):
        waited = {}
        for waits, fn, sem, inc in self.steps[eng_name]:
            for s_, c_ in sorted(waits.items()):
                if waited.get(s_, 0) < c_:
                    eng.wait_ge(sems[s_], c_)
                    waited[s_] = c_
            last = fn(eng)
            if inc:
                if isinstance(last, list):
                    for ins_ in last:
                        ins_.then_inc(sems[sem], 16)
                else:
                    last.then_inc(sems[sem], inc)

    def final_wait(self, eng_name, prefix):
        waits = {s_: c_ for s_, c_ in self.cnt.items()
                 if s_.startswith(prefix) and c_ > 0}
        self.steps[eng_name].append((waits, lambda e: e.nop(), ENG_SEM[eng_name], 0))



def build_nc(stage=STAGE):
    nc = bass.Bass("TRN2", target_bir_lowering=False, debug=False)
    sch = Sched()

    dt_in = {}

    def din(name, shape):
        t = nc.dram_tensor(name, shape, F32, kind="ExternalInput")
        dt_in[name] = t
        return t

    x_in = din("x_in", [128, KC * TOK])
    wq_a = din("wq_a", [KC, 128, D])
    wk_a = din("wk_a", [KC, 128, D])
    wv_a = din("wv_a", [KC, 128, D])
    wo_a = din("wo_a", [KC, 128, D])
    wq_b = din("wq_b", [KC, 128, D])
    wk_b = din("wk_b", [1, 128, D])
    wv_b = din("wv_b", [1, 128, D])
    wo_b = din("wo_b", [KC, 128, D])
    rtr = din("rtr", [128, KC * 4])
    a_pre = din("a_pre", [128, KC * 64])
    b_pre = din("b_pre", [64, D])
    ident_in = din("ident_in", [128, 128])
    ones_in = din("ones_in", [128, 1])
    onesr_in = din("onesr_in", [1, 128])
    dmask_in = din("dmask_in", [128, 128])
    e4_in = din("e4_in", [4, 64])
    eps_in = din("eps_in", [128, 1])

    out_t = nc.dram_tensor("out", [128, KC * TOK], F32, kind="ExternalOutput")
    dbg_t = None
    if stage in (0, 0.5):
        dbg_t = nc.dram_tensor("dbg", [3 * RPB * TOK, TOK], F32, kind="ExternalOutput")
    if stage == 0.8:
        dbg_t = nc.dram_tensor("dbg", [HPC * DK, SEQ], F32, kind="ExternalOutput")
    if stage == 1.5:
        dbg_t = nc.dram_tensor("dbg", [260, TOK], F32, kind="ExternalOutput")

    qkv_in = nc.dram_tensor("qkv_in", [3 * D, TOK], BF16)
    qkv_g = nc.dram_tensor("qkv_g", [RPB * 3 * D, TOK], BF16)
    part_g = [nc.dram_tensor(f"part_g{i}", [RPB * D, TOK], BF16) for i in range(3)]
    o_in = nc.dram_tensor("o_in", [HPC * DK, SEQ], BF16)
    o_g = nc.dram_tensor("o_g", [RPB * HPC * DK, SEQ], BF16)
    kvb_in = nc.dram_tensor("kvb_in", [DK + TOK, max(TOK, DK)], BF16)
    kvb_g = nc.dram_tensor("kvb_g", [RPB * (DK + TOK), max(TOK, DK)], BF16)
    q_in_b = nc.dram_tensor("q_in_b", [D, TOK], BF16)
    q_g_b = nc.dram_tensor("q_g_b", [RPB * D, TOK], BF16)
    qkv_m = nc.dram_tensor("qkv_m", [RPB * 3 * TOK, TOK], BF16)
    qb_m = nc.dram_tensor("qb_m", [RPB * TOK, TOK], BF16)
    o_loc = nc.dram_tensor("o_loc", [RPB * HPC * DK, TOK], BF16)

    RG = [[0, 1, 2, 3], [4, 5, 6, 7]]

    _sym_cache = {}

    def sym_off(e, const, coef):
        k = (id(e), const, coef)
        if k not in _sym_cache:
            kp = (id(e), "pid")
            if kp not in _sym_cache:
                _sym_cache[kp] = e.partition_id() % 4
            _sym_cache[k] = _sym_cache[kp] * coef + const
        return _sym_cache[k]

    with contextlib.ExitStack() as st:
        E = st.enter_context

        sems = {}
        for s_ in ["t", "v", "s", "gc", "yc"]:
            sems[s_] = E(nc.semaphore("sem_" + s_))
        for base, nr in Sched.NROT.items():
            for i_ in range(nr):
                sems[f"{base}{i_}"] = E(nc.semaphore(f"sem_{base}{i_}"))

        x_sb = E(nc.sbuf_tensor("x_sb", [128, KC * TOK], BF16))
        a1 = E(nc.sbuf_tensor("a1", [128, 48128], BF16))
        ident_bf = E(nc.sbuf_tensor("ident_bf", [128, 128], BF16))
        ident_f = E(nc.sbuf_tensor("ident_f", [128, 128], F32))
        ones_bf = E(nc.sbuf_tensor("ones_bf", [128, 1], BF16))
        onesr_f = E(nc.sbuf_tensor("onesr_f", [1, 128], F32))
        dmask = E(nc.sbuf_tensor("dmask", [128, 128], F32))
        e4_sb = E(nc.sbuf_tensor("e4_sb", [4, 64], BF16))
        eps_sb = E(nc.sbuf_tensor("eps_sb", [128, 1], F32))
        rtr_sb = E(nc.sbuf_tensor("rtr_sb", [128, KC * 4], BF16))
        a_sb = E(nc.sbuf_tensor("a_sb", [128, KC * 64], BF16))
        b_sb = E(nc.sbuf_tensor("b_sb", [64, D], BF16))
        st1 = E(nc.sbuf_tensor("st1", [1, 512], F32))
        st2 = E(nc.sbuf_tensor("st2", [1, 512], F32))
        st3 = E(nc.sbuf_tensor("st3", [1, 512], F32))
        st4 = E(nc.sbuf_tensor("st4", [1, 512], F32))
        nmst = E(nc.sbuf_tensor("nmst", [128, 16], F32))
        lst = E(nc.sbuf_tensor("lst", [128, 16], F32))
        alst = E(nc.sbuf_tensor("alst", [128, 16], F32))
        sgm = E(nc.sbuf_tensor("sgm", [128, 4], F32))
        sgl = E(nc.sbuf_tensor("sgl", [128, 4], F32))
        sm1 = E(nc.sbuf_tensor("sm1", [128, 2], F32))
        sm2 = E(nc.sbuf_tensor("sm2", [128, 2], F32))
        lgf = E(nc.sbuf_tensor("lgf", [4, TOK], F32))
        lgt = E(nc.sbuf_tensor("lgt", [128, 32], F32))
        gt1 = E(nc.sbuf_tensor("gt1", [128, 4], F32))
        gt2 = E(nc.sbuf_tensor("gt2", [128, 4], F32))
        gtm = E(nc.sbuf_tensor("gtm", [128, 32], F32))
        gfm = E(nc.sbuf_tensor("gfm", [4, TOK], BF16))
        mid_sb = E(nc.sbuf_tensor("mid_sb", [64, TOK], BF16))
        midw = E(nc.sbuf_tensor("midw", [64, TOK], BF16))
        fm1 = E(nc.sbuf_tensor("fm1", [128, 1], F32))
        fm2 = E(nc.sbuf_tensor("fm2", [128, 1], F32))
        fz = E(nc.sbuf_tensor("fz", [128, 1], F32))

        ps = [E(nc.psum_tensor(f"ps{i}", [128, 512], F32)) for i in range(4)]
        psT_p = [E(nc.psum_tensor(f"psT{i}", [128, 1024], BF16)) for i in range(2)]
        psO_p = [E(nc.psum_tensor(f"psO{i}", [128, 512], F32)) for i in range(2)]
        psT = psT_p[0]
        psO = psO_p[0]

        PGSZ = 512

        def view(start, nel):
            assert start + nel <= 48128, (start, nel)
            return a1[:, start:start + nel], ("a1", start // PGSZ, -(-(start + nel) // PGSZ))

        def pages(v):
            _, p0, p1 = v[1]
            return [("a1", i) for i in range(p0, p1)]

        XN = 0
        xn_v, _ = view(XN, 32768)

        def xn_pages(kcs=None):
            if kcs is None:
                return [("a1", i) for i in range(0, 64)]
            out = []
            for kc_ in kcs:
                out += [("a1", kc_ * 2), ("a1", kc_ * 2 + 1)]
            return out

        WS = 32768
        wslot = [view(WS + i * 4096, 4096)[0] for i in range(3)]

        def w_pages(i):
            b = (WS + i * 4096) // PGSZ
            return [("a1", b + j) for j in range(8)]

        STG = WS + 3 * 4096
        stg = [view(STG + i * 1024, 1024)[0] for i in range(2)]

        def stg_pages(i):
            b = (STG + i * 1024) // PGSZ
            return [("a1", b), ("a1", b + 1)]

        XSQ = STG + 2048
        xsq = [view(XSQ + i * 512, 512)[0] for i in range(2)]

        def xsq_pages(i):
            return [("a1", (XSQ + i * 512) // PGSZ)]

        K0 = 0
        k_sb = [view(K0 + i * 4096, 4096)[0] for i in range(2)]

        def k_pages(i):
            b = (K0 + i * 4096) // PGSZ
            return [("a1", b + j) for j in range(8)]

        V0 = 8192
        v_sb = [view(V0 + i * 4096, 4096)[0] for i in range(2)]

        def v_pages(i):
            b = (V0 + i * 4096) // PGSZ
            return [("a1", b + j) for j in range(8)]

        P0 = 16384
        p_sb, _ = view(P0, 4096)

        def p_pages():
            return [("a1", P0 // PGSZ + j) for j in range(8)]

        PT0 = 20480
        pt_sb, _ = view(PT0, 4096)

        def pt_pages():
            return [("a1", PT0 // PGSZ + j) for j in range(8)]

        AOH = 24576
        aoh_sb, _ = view(AOH, 4096)

        def aoh_pages():
            return [("a1", AOH // PGSZ + j) for j in range(8)]

        VT0 = 28672
        vtmp = [view(VT0 + i * 1024, 1024)[0] for i in range(2)]

        def vt_pages(i):
            b = (VT0 + i * 1024) // PGSZ
            return [("a1", b), ("a1", b + 1)]

        Q0 = 30720
        q_sb = [view(Q0 + i * 512, 512)[0] for i in range(2)]

        def q_pages(i):
            return [("a1", (Q0 + i * 512) // PGSZ)]

        def psp(i):
            return [("ps", i)]

        def dma(eng, out_ap, in_ap, reads, writes, ni=1):
            sem = "gd" if eng == "g" else "yd"

            def fn(e, o=out_ap, i=in_ap):
                return e.dma_start(out=o, in_=i)
            sch.step(eng, fn, reads=reads, writes=writes, sem=sem, inc=16, ni=ni)

        def dma_multi(eng, pairs, reads, writes):
            sem = "gd" if eng == "g" else "yd"

            def fn(e, pairs=pairs):
                return [e.dma_start(out=o, in_=i) for o, i in pairs]
            sch.step(eng, fn, reads=reads, writes=writes, sem=sem,
                     inc=16 * len(pairs), ni=len(pairs))

        dma("g", ident_bf[:, :], ident_in[:, :], [], [("c", "id")])
        dma("g", ident_f[:, :], ident_in[:, :], [], [("c", "idf")])
        dma("g", ones_bf[:, :], ones_in[:, :], [], [("c", "ones")])
        dma("g", onesr_f[:, :], onesr_in[:, :], [], [("c", "onesr")])
        dma("g", dmask[:, :], dmask_in[:, :], [], [("c", "dm")])
        dma("g", e4_sb[:, :], e4_in[:, :], [], [("c", "e4")])
        dma("g", eps_sb[:, :], eps_in[:, :], [], [("c", "eps")])
        dma("g", rtr_sb[:, :], rtr[:, :], [], [("c", "rtr")])
        dma("g", a_sb[:, :], a_pre[:, :], [], [("c", "a")])
        dma("g", b_sb[:, :], b_pre[:, :], [], [("c", "b")])
        for i in range(4):
            dma("g", x_sb[:, i * 8192:(i + 1) * 8192],
                x_in[:, i * 8192:(i + 1) * 8192], [],
                [("x", kc_) for kc_ in range(i * 8, (i + 1) * 8)])

        def emit_ln():
            for ct in range(2):
                cs = ct * 512
                for kc_ in range(KC):
                    par = kc_ % 2

                    def sq_fn(e, kc_=kc_, par=par, cs=cs):
                        return e.activation(xsq[par][:, :], x_sb[:, kc_ * TOK + cs: kc_ * TOK + cs + 512], AF.Square)
                    sch.step("s", sq_fn, reads=[("x", kc_)], writes=xsq_pages(par))

                    def mm_fn(e, kc_=kc_, par=par, cs=cs):
                        e.matmul(ps[2][0:1, :], ones_bf[:, :],
                                 x_sb[:, kc_ * TOK + cs: kc_ * TOK + cs + 512],
                                 start=(kc_ == 0), stop=(kc_ == KC - 1))
                        return e.matmul(ps[3][0:1, :], ones_bf[:, :], xsq[par][:, :],
                                        start=(kc_ == 0), stop=(kc_ == KC - 1))
                    sch.step("t", mm_fn, reads=[("x", kc_), ("c", "ones")] + xsq_pages(par),
                             writes=psp(2) + psp(3), ni=2)
                def stats_v1(e, cs=cs):
                    e.tensor_scalar(st1[:, :], ps[2][0:1, :], 1.0 / D, None, ALU.mult)
                    return e.tensor_scalar(st2[:, :], ps[3][0:1, :], 1.0 / D, None, ALU.mult)
                sch.step("v", stats_v1, reads=psp(2) + psp(3), writes=[("st", 1), ("st", 2)], ni=2)

                def stats_msq(e):
                    return e.tensor_mul(st3[:, :], st1[:, :], st1[:, :])
                sch.step("v", stats_msq, reads=[("st", 1)], writes=[("st", 3)])

                def stats_var(e):
                    return e.tensor_sub(st2[:, :], st2[:, :], st3[:, :])
                sch.step("v", stats_var, reads=[("st", 2), ("st", 3)], writes=[("st", 2)])

                def stats_sqrt(e):
                    return e.activation(st2[:, :], st2[:, :], AF.Sqrt, bias=eps_sb[0:1, 0:1])
                sch.step("s", stats_sqrt, reads=[("st", 2), ("c", "eps")], writes=[("st", 2)])

                def stats_rstd(e):
                    return e.reciprocal(st3[:, :], st2[:, :])
                sch.step("v", stats_rstd, reads=[("st", 2)], writes=[("st", 3)])

                def stats_nmr(e):
                    return e.scalar_tensor_tensor(st4[:, :], st1[:, :], -1.0, st3[:, :],
                                                  ALU.mult, ALU.mult)
                sch.step("v", stats_nmr, reads=[("st", 1), ("st", 3)], writes=[("st", 4)])

                def bcast(e):
                    e.matmul(ps[2][:, :], onesr_f[:, :], st3[:, :], start=True, stop=True)
                    return e.matmul(ps[3][:, :], onesr_f[:, :], st4[:, :], start=True, stop=True)
                sch.step("t", bcast, reads=[("st", 3), ("st", 4), ("c", "onesr")],
                         writes=psp(2) + psp(3), ni=2)

                for kc_ in range(KC):
                    def xn_f1(e, kc_=kc_, cs=cs):
                        dst = xn_v[:, kc_ * TOK + cs: kc_ * TOK + cs + 512]
                        return e.tensor_tensor(dst, x_sb[:, kc_ * TOK + cs: kc_ * TOK + cs + 512],
                                               ps[2][:, :], ALU.mult)
                    sch.step("v", xn_f1, reads=[("x", kc_)] + psp(2),
                             writes=[("a1", kc_ * 2 + ct)])

                    def xn_f2(e, kc_=kc_, cs=cs):
                        dst = xn_v[:, kc_ * TOK + cs: kc_ * TOK + cs + 512]
                        return e.tensor_tensor(dst, dst, ps[3][:, :], ALU.add)
                    sch.step("v", xn_f2, reads=psp(3) + [("a1", kc_ * 2 + ct)],
                             writes=[("a1", kc_ * 2 + ct)])

        def emit_proj(wdram, jmax, dest_fn, scale=None, wslot_off=0):
            for j in range(jmax):
                slot = (j + wslot_off) % 3
                dma("g", wslot[slot][:, :], bass.AP(wdram, j * 128 * D, [[D, 128], [1, D]]),
                    [], w_pages(slot))
                for ct in range(2):
                    pb = (2 * j + ct) % 2
                    cs = ct * 512

                    def mm_fn(e, j=j, slot=slot, pb=pb, cs=cs):
                        last = None
                        for kc_ in range(KC):
                            last = e.matmul(ps[pb][:, :],
                                            wslot[slot][:, kc_ * 128:(kc_ + 1) * 128],
                                            xn_v[:, kc_ * TOK + cs: kc_ * TOK + cs + 512],
                                            start=(kc_ == 0), stop=(kc_ == KC - 1))
                        return last
                    sch.step("t", mm_fn,
                             reads=w_pages(slot) + [("a1", kc_ * 2 + ct) for kc_ in range(KC)],
                             writes=psp(pb), ni=KC)
                    dest_fn(j, ct, pb)

        def stage_to_dram(dram_t, row_off, scale):
            def dest(j, ct, pb, dram_t=dram_t, row_off=row_off, scale=scale):
                spar = j % 2
                cs = ct * 512

                def evac(e, spar=spar, cs=cs, pb=pb):
                    if scale is None:
                        return e.tensor_copy(stg[spar][:, cs:cs + 512], ps[pb][:, :])
                    return e.tensor_scalar(stg[spar][:, cs:cs + 512], ps[pb][:, :],
                                           float(scale), None, ALU.mult)
                sch.step("v", evac, reads=psp(pb), writes=stg_pages(spar))
                if ct == 1:
                    dma("y", bass.AP(dram_t, (row_off + j * 128) * TOK, [[TOK, 128], [1, TOK]]),
                        stg[spar][:, :], stg_pages(spar), [("d", dram_t.name, row_off + j * 128)])
            return dest

        def emit_collective(in_t, out_t2, tag, nrows_in):
            def fn(e, in_t=in_t, out_t2=out_t2):
                return e.collective_compute(
                    "AllGather", ALU.bypass, replica_groups=RG,
                    ins=[in_t.ap().opt()], outs=[out_t2.ap().opt()])
            reads = [("d", in_t.name, r) for r in range(0, nrows_in, 128)]
            sch.step("g", fn, reads=reads, writes=[("d", out_t2.name, 0)], sem="cc", inc=1)

        def emit_collective_rows(in_t, row0, out_t2, tag):
            def fn(e, in_t=in_t, row0=row0, out_t2=out_t2):
                return e.collective_compute(
                    "AllGather", ALU.bypass, replica_groups=RG,
                    ins=[bass.AP(in_t, row0 * TOK, [[TOK, D], [1, TOK]]).opt()],
                    outs=[out_t2.ap().opt()])
            reads = [("d", in_t.name, row0 + r) for r in range(0, D, 128)]
            sch.step("g", fn, reads=reads, writes=[("d", out_t2.name, 0)], sem="cc", inc=1)

        def emit_mirror(src_t, dst_t, parts):
            def fn(e, src_t=src_t, dst_t=dst_t, parts=parts):
                base = sym_off(e, 0, HPC * DK * TOK)
                src = bass.AP(src_t, base,
                              [[parts * D * TOK, RPB], [D * TOK, parts],
                               [TOK * TOK // 16, 16], [1, TOK * TOK // 16]])
                dst = bass.AP(dst_t, 0,
                              [[parts * TOK * TOK, RPB], [TOK * TOK, parts],
                               [TOK * TOK // 16, 16], [1, TOK * TOK // 16]])
                return e.dma_start(out=dst, in_=src)
            sch.step("y", fn, reads=[("d", src_t.name, 0)],
                     writes=[("d", dst_t.name, 0)], sem="yd", inc=16)

        def emit_mirror_part(src_t, part):
            def fn(e, src_t=src_t, part=part):
                base = sym_off(e, 0, HPC * DK * TOK)
                src = bass.AP(src_t, base,
                              [[D * TOK, RPB], [TOK * TOK // 16, 16], [1, TOK * TOK // 16]])
                dst = bass.AP(qkv_m, part * TOK * TOK,
                              [[3 * TOK * TOK, RPB], [TOK * TOK // 16, 16], [1, TOK * TOK // 16]])
                return e.dma_start(out=dst, in_=src)
            sch.step("y", fn, reads=[("d", src_t.name, 0)],
                     writes=[("d", "qkv_m", part)], sem="yd", inc=16)

        def emit_mirror_o(src_t):
            for r4 in range(RPB):
                def fn(e, r4=r4, src_t=src_t):
                    base = sym_off(e, r4 * TOK * SEQ, TOK)
                    src = bass.AP(src_t, base, [[SEQ, TOK], [1, TOK]])
                    dst = bass.AP(o_loc, r4 * TOK * TOK, [[TOK, TOK], [1, TOK]])
                    return e.dma_start(out=dst, in_=src)
                sch.step("y", fn, reads=[("d", src_t.name, 0)],
                         writes=[("d", "o_loc", 0)], sem="yd", inc=16)

        def emit_attention(variant):
            aidx = [0]
            sbctr = [0]
            pending = []

            if variant == "a":
                n_pass = 4
            else:
                WID = max(TOK, DK)
                kpairs, vpairs = [], []
                for r in range(RPB):
                    base = r * (DK + TOK) * WID
                    kpairs.append((k_sb[0][:, r * 1024:(r + 1) * 1024],
                                   bass.AP(kvb_g, base, [[WID, 128], [1, TOK]])))
                    vpairs.append((v_sb[0][:, r * 1024:(r + 1) * 1024],
                                   bass.AP(kvb_g, base + DK * WID,
                                           [[WID, 128], [WID * 128, 8], [1, DK]])))
                dma_multi("y", kpairs, [("d", "kvb_g", 0)], k_pages(0))
                dma_multi("y", vpairs, [("d", "kvb_g", 0)], v_pages(0))
                n_pass = 8

            for pp in range(n_pass):
                if variant == "a":
                    heads = [2 * pp, 2 * pp + 1]
                    for hh, h_ in enumerate(heads):
                        kpairs = []
                        for r in range(RPB):
                            kpairs.append(
                                (k_sb[hh][:, r * 1024:(r + 1) * 1024],
                                 bass.AP(qkv_m, ((r * 3 + 1) * TOK + h_ * DK) * TOK,
                                         [[TOK, 128], [1, TOK]])))

                        def kload(e, kpairs=kpairs):
                            return [e.dma_start(out=o, in_=i) for o, i in kpairs]
                        sch.step("y", kload,
                                 reads=[("d", "qkv_m", 1), ("d", "qkv_m", 0)],
                                 writes=k_pages(hh), sem="yd", inc=16 * RPB, ni=4)
                        for r in range(RPB):
                            vp = r % 2

                            def vload(e, r=r, h_=h_, vp=vp):
                                src = bass.AP(qkv_m, ((r * 3 + 2) * TOK + h_ * DK) * TOK,
                                              [[TOK, 128], [1, TOK]])
                                return e.dma_start(out=vtmp[vp][:, :], in_=src)
                            sch.step("y", vload, reads=[("d", "qkv_m", 2)],
                                     writes=vt_pages(vp), sem="yd", inc=16)

                            def vtr(e, vp=vp):
                                last = None
                                for tt in range(8):
                                    last = e.transpose(psT[:, tt * 128:(tt + 1) * 128],
                                                       vtmp[vp][:, tt * 128:(tt + 1) * 128],
                                                       ident_bf[:, :])
                                return last
                            sch.step("t", vtr, reads=vt_pages(vp) + [("c", "id")],
                                     writes=[("ps", "T", 0)], ni=8)

                            def vev(e, hh=hh, r=r):
                                return e.tensor_copy(
                                    v_sb[hh][:, r * 1024:(r + 1) * 1024], psT[:, :])
                            sch.step("v", vev, reads=[("ps", "T", 0)], writes=v_pages(hh))
                    local_heads = [(0, 0, heads[0]), (1, 1, heads[1])]
                else:
                    local_heads = [(0, 0, pp)]

                for kvp, hh, h_ in local_heads:
                    for qcv in range(8):
                        qpar = qcv % 2

                        def qload(e, h_=h_, qcv=qcv, qpar=qpar, variant=variant):
                            sec = qcv // 2
                            if variant == "a":
                                src = bass.AP(qkv_m,
                                              (sec * 3 * TOK + h_ * DK) * TOK + (qcv % 2) * 512,
                                              [[TOK, 128], [1, 512]])
                            else:
                                src = bass.AP(qb_m,
                                              (sec * TOK + h_ * DK) * TOK + (qcv % 2) * 512,
                                              [[TOK, 128], [1, 512]])
                            return e.dma_start(out=q_sb[qpar][:, :], in_=src)
                        sch.step("y", qload,
                                 reads=[("d", "qkv_m" if variant == "a" else "qb_m", 0)],
                                 writes=q_pages(qpar), sem="yd", inc=16)

                        for qt in range(4):
                            g = qcv * 4 + qt
                            nb = g + 1
                            span = nb * 128
                            ngrp = (span + 511) // 512
                            aidx[0] += 1
                            par = aidx[0] % 2
                            pB = 16384 if par == 0 else 32768
                            ptB = 20480 if par == 0 else 36864
                            pPg = pB // PGSZ
                            ptPg = ptB // PGSZ
                            nmv = nmst[:, par * 8:par * 8 + 8]
                            lsv = lst[:, par * 8:par * 8 + 8]
                            alv = alst[:, par * 8:par * 8 + 8]
                            sgmv = sgm[:, par * 2:par * 2 + 2]
                            sglv = sgl[:, par * 2:par * 2 + 2]
                            sm1v = sm1[:, par:par + 1]
                            sm2v = sm2[:, par:par + 1]
                            K_NM, K_L, K_AL = ("st", "nm", par), ("st", "l", par), ("st", "al", par)
                            K_SGM, K_SGL = ("st", "sgm", par), ("st", "sgl", par)
                            K_GM, K_LS = ("st", "gm", par), ("st", "lsum", par)
                            nsg = (ngrp + 3) // 4
                            assert nsg <= 2
                            banks = {}
                            for sg in range(nsg):
                                glo, ghi = sg * 4, min(sg * 4 + 4, ngrp)
                                for gi in range(glo, ghi):
                                    w_ = min(512, span - gi * 512)
                                    bank = sbctr[0] % 4
                                    sbctr[0] += 1
                                    banks[gi] = bank

                                    def smm(e, gi=gi, w_=w_, bank=bank, qt=qt, kvp=kvp,
                                            qpar=qpar):
                                        return e.matmul(
                                            ps[bank][:, 0:w_],
                                            q_sb[qpar][:, qt * 128:(qt + 1) * 128],
                                            k_sb[kvp][:, gi * 512: gi * 512 + w_],
                                            start=True, stop=True)
                                    sch.step("t", smm, reads=q_pages(qpar) + k_pages(kvp),
                                             writes=psp(bank))
                                    if gi == ngrp - 1:
                                        dcol = (nb - 1) * 128 - gi * 512

                                        def dmsk(e, bank=bank, dcol=dcol):
                                            return e.tensor_tensor(ps[bank][:, dcol:dcol + 128],
                                                                   ps[bank][:, dcol:dcol + 128],
                                                                   dmask[:, :], ALU.add)
                                        sch.step("v", dmsk, reads=psp(bank) + [("c", "dm")],
                                                 writes=psp(bank))

                                    def rmax(e, gi=gi, w_=w_, bank=bank, nmv=nmv):
                                        return e.reduce_max(nmv[:, gi:gi + 1], ps[bank][:, 0:w_],
                                                            axis=AX.X, negate=True)
                                    sch.step("v", rmax, reads=psp(bank), writes=[K_NM])
                                if ghi - glo > 1:
                                    def sgmin(e, glo=glo, ghi=ghi, sg=sg, nmv=nmv, sgmv=sgmv):
                                        return e.tensor_reduce(sgmv[:, sg:sg + 1],
                                                               nmv[:, glo:ghi], AX.X, ALU.min)
                                    sch.step("v", sgmin, reads=[K_NM], writes=[K_SGM])
                                else:
                                    def sgcp(e, glo=glo, sg=sg, nmv=nmv, sgmv=sgmv):
                                        return e.tensor_copy(sgmv[:, sg:sg + 1], nmv[:, glo:glo + 1])
                                    sch.step("v", sgcp, reads=[K_NM], writes=[K_SGM])
                                for gi in range(glo, ghi):
                                    w_ = min(512, span - gi * 512)
                                    bank = banks[gi]

                                    def expf(e, gi=gi, w_=w_, bank=bank, sg=sg, pB=pB,
                                             sgmv=sgmv, lsv=lsv):
                                        return e.activation(a1[:, pB + gi * 512: pB + gi * 512 + w_],
                                                            ps[bank][:, 0:w_], AF.Exp,
                                                            bias=sgmv[:, sg:sg + 1],
                                                            accum_out=lsv[:, gi:gi + 1])
                                    sch.step("s", expf, reads=psp(bank) + [K_SGM],
                                             writes=[("a1", pPg + gi), K_L])
                            for sg in range(nsg):
                                glo, ghi = sg * 4, min(sg * 4 + 4, ngrp)
                                if ghi - glo > 1:
                                    def lsumf(e, glo=glo, ghi=ghi, sg=sg, lsv=lsv, sglv=sglv):
                                        return e.reduce_sum(sglv[:, sg:sg + 1], lsv[:, glo:ghi],
                                                            axis=AX.X)
                                    sch.step("v", lsumf, reads=[K_L], writes=[K_SGL])
                                else:
                                    def lcp(e, glo=glo, sg=sg, lsv=lsv, sglv=sglv):
                                        return e.tensor_copy(sglv[:, sg:sg + 1], lsv[:, glo:glo + 1])
                                    sch.step("v", lcp, reads=[K_L], writes=[K_SGL])
                            if nsg > 1:
                                def gmin(e, nsg=nsg, sgmv=sgmv, sm1v=sm1v):
                                    return e.tensor_reduce(sm1v, sgmv[:, 0:nsg], AX.X, ALU.min)
                                sch.step("v", gmin, reads=[K_SGM], writes=[K_GM])

                                def aarg(e, nsg=nsg, sgmv=sgmv, sm1v=sm1v, alv=alv):
                                    return e.tensor_scalar(alv[:, 0:nsg], sgmv[:, 0:nsg],
                                                           -1.0, sm1v, ALU.mult, ALU.add)
                                sch.step("v", aarg, reads=[K_SGM, K_GM], writes=[K_AL])

                                def aexp(e, nsg=nsg, alv=alv):
                                    return e.activation(alv[:, 0:nsg], alv[:, 0:nsg], AF.Exp)
                                sch.step("s", aexp, reads=[K_AL], writes=[K_AL])

                                def lw_f(e, nsg=nsg, alv=alv, sglv=sglv):
                                    return e.tensor_mul(sglv[:, 0:nsg], alv[:, 0:nsg], sglv[:, 0:nsg])
                                sch.step("v", lw_f, reads=[K_AL, K_SGL], writes=[K_SGL])

                                def lsum2(e, nsg=nsg, sglv=sglv, sm2v=sm2v):
                                    return e.reduce_sum(sm2v, sglv[:, 0:nsg], axis=AX.X)
                                sch.step("v", lsum2, reads=[K_SGL], writes=[K_LS])

                                def lrec(e, sm2v=sm2v):
                                    return e.reciprocal(sm2v, sm2v)
                                sch.step("v", lrec, reads=[K_LS], writes=[K_LS])

                                def coef_f(e, nsg=nsg, alv=alv, sm2v=sm2v):
                                    return e.tensor_scalar(alv[:, 0:nsg], alv[:, 0:nsg],
                                                           sm2v, None, ALU.mult)
                                sch.step("v", coef_f, reads=[K_AL, K_LS], writes=[K_AL])
                                for gi in range(ngrp):
                                    w_ = min(512, span - gi * 512)
                                    sg = gi // 4

                                    def pnorm(e, gi=gi, w_=w_, sg=sg, pB=pB, alv=alv):
                                        return e.tensor_scalar(a1[:, pB + gi * 512: pB + gi * 512 + w_],
                                                               a1[:, pB + gi * 512: pB + gi * 512 + w_],
                                                               alv[:, sg:sg + 1], None, ALU.mult)
                                    sch.step("v", pnorm,
                                             reads=[("a1", pPg + gi), K_AL],
                                             writes=[("a1", pPg + gi)])
                            else:
                                def rle(e, sglv=sglv, sm2v=sm2v):
                                    return e.reciprocal(sm2v, sglv[:, 0:1])
                                sch.step("v", rle, reads=[K_SGL], writes=[K_LS])
                                for gi in range(ngrp):
                                    w_ = min(512, span - gi * 512)

                                    def pnorm(e, gi=gi, w_=w_, pB=pB, sm2v=sm2v):
                                        return e.tensor_scalar(a1[:, pB + gi * 512: pB + gi * 512 + w_],
                                                               a1[:, pB + gi * 512: pB + gi * 512 + w_],
                                                               sm2v, None, ALU.mult)
                                    sch.step("v", pnorm,
                                             reads=[("a1", pPg + gi), K_LS],
                                             writes=[("a1", pPg + gi)])
                            def phase2(nb=nb, kvp=kvp, qt=qt, qcv=qcv, pB=pB,
                                       ptB=ptB, pPg=pPg, ptPg=ptPg, par=par):
                                for kb0 in range(0, nb, 8):
                                    kbn = min(8, nb - kb0)

                                    def trf(e, kb0=kb0, kbn=kbn, pB=pB, par=par):
                                        last = None
                                        for k_ in range(kbn):
                                            last = e.transpose(psT_p[par][:, k_ * 128:(k_ + 1) * 128],
                                                               a1[:, pB + (kb0 + k_) * 128: pB + (kb0 + k_ + 1) * 128],
                                                               ident_bf[:, :])
                                        return last
                                    sch.step("t", trf,
                                             reads=[("a1", pPg + (kb0 // 4) + j) for j in range(2)] + [("c", "id")],
                                             writes=[("ps", "T", par)], ni=kbn)

                                    def tev(e, kb0=kb0, kbn=kbn, ptB=ptB, par=par):
                                        return e.tensor_copy(a1[:, ptB + kb0 * 128: ptB + (kb0 + kbn) * 128],
                                                             psT_p[par][:, 0:kbn * 128])
                                    sch.step("v", tev, reads=[("ps", "T", par)],
                                             writes=[("a1", ptPg + kb0 // 4),
                                                     ("a1", ptPg + kb0 // 4 + 1)])

                                def pv(e, nb=nb, kvp=kvp, qt=qt, ptB=ptB, par=par):
                                    last = None
                                    for kb in range(nb):
                                        last = e.matmul(psO_p[par][:, (qt % 4) * 128:(qt % 4) * 128 + 128],
                                                        v_sb[kvp][:, kb * 128: (kb + 1) * 128],
                                                        a1[:, ptB + kb * 128: ptB + (kb + 1) * 128],
                                                        start=(kb == 0), stop=(kb == nb - 1))
                                    return last
                                sch.step("t", pv,
                                         reads=[("a1", ptPg + j) for j in range(8)] + v_pages(kvp),
                                         writes=[("ps", "O", par)], ni=nb)

                                def oev(e, qcv=qcv, qt=qt, par=par):
                                    return e.tensor_copy(aoh_sb[:, qcv * 512 + qt * 128: qcv * 512 + (qt + 1) * 128],
                                                         psO_p[par][:, (qt % 4) * 128:(qt % 4) * 128 + 128])
                                sch.step("v", oev, reads=[("ps", "O", par)], writes=aoh_pages())
                            pending.append(phase2)
                            if len(pending) > 1:
                                pending.pop(0)()
                    while pending:
                        pending.pop(0)()
                    myh = h_
                    opairs = []
                    for r in range(RPB):
                        opairs.append((bass.AP(o_in, (myh * DK) * SEQ + r * 1024, [[SEQ, 128], [1, 1024]]),
                                       aoh_sb[:, r * 1024:(r + 1) * 1024]))
                    dma_multi("y", opairs, aoh_pages(), [("d", "o_in", myh * 128)])

        def emit_outproj(wdram, gathered, wslot_off=0):
            for kc_ in range(KC):
                def afm_load(e, kc_=kc_):
                    src = bass.AP(o_loc, (kc_ * 128) * TOK, [[TOK, 128], [1, TOK]])
                    return e.dma_start(out=xn_v[:, kc_ * TOK:(kc_ + 1) * TOK], in_=src)
                sch.step("y", afm_load, reads=[("d", "o_loc", 0)],
                         writes=[("a1", kc_ * 2), ("a1", kc_ * 2 + 1)], sem="yd", inc=16)
            for j in range(KC):
                slot = (j + wslot_off) % 3
                dma("g", wslot[slot][:, :], bass.AP(wdram, j * 128 * D, [[D, 128], [1, D]]),
                    [], w_pages(slot))
                for ct in range(2):
                    pb = (2 * j + ct) % 2
                    cs = ct * 512

                    def mm_fn(e, j=j, slot=slot, pb=pb, cs=cs):
                        last = None
                        for kc_ in range(KC):
                            last = e.matmul(ps[pb][:, :],
                                            wslot[slot][:, kc_ * 128:(kc_ + 1) * 128],
                                            xn_v[:, kc_ * TOK + cs: kc_ * TOK + cs + 512],
                                            start=(kc_ == 0), stop=(kc_ == KC - 1))
                        return last
                    sch.step("t", mm_fn,
                             reads=w_pages(slot) + [("a1", kc_ * 2 + ct) for kc_ in range(KC)],
                             writes=psp(pb), ni=KC)

                    def radd(e, j=j, pb=pb, cs=cs):
                        dst = x_sb[:, j * TOK + cs: j * TOK + cs + 512]
                        return e.tensor_tensor(dst, dst, ps[pb][:, :], ALU.add)
                    sch.step("v", radd, reads=psp(pb) + [("x", j)], writes=[("x", j)])

        def emit_ffn():
            for ct in range(2):
                cs = ct * 512

                def lmm(e, cs=cs):
                    last = None
                    for kc_ in range(KC):
                        last = e.matmul(ps[2][0:4, :], rtr_sb[:, kc_ * 4:(kc_ + 1) * 4],
                                        xn_v[:, kc_ * TOK + cs: kc_ * TOK + cs + 512],
                                        start=(kc_ == 0), stop=(kc_ == KC - 1))
                    return last
                sch.step("t", lmm, reads=[("c", "rtr")] + [("a1", kc_ * 2 + ct) for kc_ in range(KC)],
                         writes=psp(2), ni=KC)

                def lev(e, cs=cs):
                    return e.tensor_copy(lgf[:, cs:cs + 512], ps[2][0:4, :])
                sch.step("v", lev, reads=psp(2), writes=[("st", "lgf")])
            def ltr(e):
                last = None
                for tt in range(8):
                    last = e.transpose(psO[:, tt * 4:(tt + 1) * 4],
                                       lgf[0:4, tt * 128:(tt + 1) * 128],
                                       ident_f[0:4, 0:4])
                return last
            sch.step("t", ltr, reads=[("st", "lgf"), ("c", "idf")], writes=[("ps", "O", 0)], ni=8)

            def ltv(e):
                return e.tensor_copy(lgt[:, 0:32], psO[:, 0:32])
            sch.step("v", ltv, reads=[("ps", "O", 0)], writes=[("st", "lgt")])

            for tt in range(8):
                sl = lgt[:, tt * 4:(tt + 1) * 4]

                def g1(e, sl=sl):
                    e.reduce_max(fm1[:, :], sl, axis=AX.X)
                    return e.reduce_max(fm2[:, :], sl, axis=AX.X, negate=True)
                sch.step("v", g1, reads=[("st", "lgt")], writes=[("st", "fm1"), ("st", "fm2")], ni=2)

                def g2(e, sl=sl):
                    return e.tensor_scalar(gt1[:, :], sl, fm1[:, 0:1], None, ALU.is_equal)
                sch.step("v", g2, reads=[("st", "lgt"), ("st", "fm1")], writes=[("st", "gt1")])

                def g3(e, sl=sl):
                    return e.scalar_tensor_tensor(gt2[:, :], gt1[:, :], MASKV, sl, ALU.mult, ALU.add)
                sch.step("v", g3, reads=[("st", "gt1"), ("st", "lgt")], writes=[("st", "gt2")])

                def g4(e):
                    return e.reduce_max(fz[:, :], gt2[:, :], axis=AX.X)
                sch.step("v", g4, reads=[("st", "gt2")], writes=[("st", "fz")])

                def g5(e, sl=sl):
                    return e.tensor_scalar(gt1[:, :], sl, fz[:, 0:1], None, ALU.is_ge)
                sch.step("v", g5, reads=[("st", "lgt"), ("st", "fz")], writes=[("st", "gt1")])

                def g6(e, sl=sl):
                    return e.activation(gt2[:, :], sl, AF.Exp, bias=fm2[:, 0:1])
                sch.step("s", g6, reads=[("st", "lgt"), ("st", "fm2")], writes=[("st", "gt2")])

                def g7a(e):
                    return e.tensor_mul(gt2[:, :], gt2[:, :], gt1[:, :])
                sch.step("v", g7a, reads=[("st", "gt1"), ("st", "gt2")], writes=[("st", "gt2")])

                def g7b(e):
                    return e.reduce_sum(fz[:, :], gt2[:, :], axis=AX.X)
                sch.step("v", g7b, reads=[("st", "gt2")], writes=[("st", "fz")])

                def g8a(e):
                    return e.reciprocal(fz[:, :], fz[:, :])
                sch.step("v", g8a, reads=[("st", "fz")], writes=[("st", "fz")])

                def g8b(e, tt=tt):
                    return e.tensor_scalar(gtm[:, tt * 4:(tt + 1) * 4], gt2[:, :],
                                           fz[:, 0:1], None, ALU.mult)
                sch.step("v", g8b, reads=[("st", "gt2"), ("st", "fz")], writes=[("st", "gtm")])

            for half in range(2):
                def gtr(e, half=half):
                    last = None
                    for tt in range(4):
                        last = e.transpose(psO[0:4, tt * 128:(tt + 1) * 128],
                                           gtm[:, (half * 4 + tt) * 4:(half * 4 + tt + 1) * 4],
                                           ident_f[:, :])
                    return last
                sch.step("t", gtr, reads=[("st", "gtm"), ("c", "idf")],
                         writes=[("ps", "O", 0)], ni=4)

                def gtv(e, half=half):
                    return e.tensor_copy(gfm[:, half * 512:(half + 1) * 512], psO[0:4, 0:512])
                sch.step("v", gtv,
                         reads=[("ps", "O", 0)],
                         writes=[("st", "gfm")])

            for ct in range(2):
                cs = ct * 512

                def midmm(e, cs=cs):
                    last = None
                    for kc_ in range(KC):
                        last = e.matmul(ps[2][0:64, :], a_sb[:, kc_ * 64:(kc_ + 1) * 64],
                                        xn_v[:, kc_ * TOK + cs: kc_ * TOK + cs + 512],
                                        start=(kc_ == 0), stop=(kc_ == KC - 1))
                    return last
                sch.step("t", midmm, reads=[("c", "a")] + [("a1", kc_ * 2 + ct) for kc_ in range(KC)],
                         writes=psp(2), ni=KC)

                def midev(e, cs=cs):
                    return e.tensor_copy(mid_sb[:, cs:cs + 512], ps[2][0:64, :])
                sch.step("v", midev, reads=psp(2), writes=[("st", "mid")])

                def g64mm(e, cs=cs):
                    return e.matmul(ps[3][0:64, :], e4_sb[:, :], gfm[:, cs:cs + 512],
                                    start=True, stop=True)
                sch.step("t", g64mm, reads=[("c", "e4"), ("st", "gfm")], writes=psp(3))

                def midw_f(e, cs=cs):
                    return e.tensor_tensor(midw[:, cs:cs + 512], mid_sb[:, cs:cs + 512],
                                           ps[3][0:64, :], ALU.mult)
                sch.step("v", midw_f, reads=[("st", "mid")] + psp(3), writes=[("st", "midw")])

            for j in range(KC):
                for ct in range(2):
                    pb = (2 * j + ct) % 2
                    cs = ct * 512

                    def dmm(e, j=j, pb=pb, cs=cs):
                        return e.matmul(ps[pb][:, :], b_sb[:, j * 128:(j + 1) * 128],
                                        midw[:, cs:cs + 512], start=True, stop=True)
                    sch.step("t", dmm, reads=[("c", "b"), ("st", "midw")], writes=psp(pb))

                    def radd(e, j=j, pb=pb, cs=cs):
                        dst = x_sb[:, j * TOK + cs: j * TOK + cs + 512]
                        return e.tensor_tensor(dst, dst, ps[pb][:, :], ALU.add)
                    sch.step("v", radd, reads=psp(pb) + [("x", j)], writes=[("x", j)])

        if stage == 0:
            emit_ln()
            for i in range(4):
                def xdmp(e, i=i):
                    return e.dma_start(out=out_t[:, i * 8192:(i + 1) * 8192],
                                       in_=xn_v[:, i * 8192:(i + 1) * 8192])
                sch.step("g", xdmp, reads=xn_pages(), writes=[("o", i)], sem="gd", inc=16)
            def ddmp0(e):
                return e.dma_start(out=dbg_t[0:128, :], in_=x_sb[:, 0:1024])
            sch.step("g", ddmp0, reads=[("x", 0)], writes=[("dbg", 0)], sem="gd", inc=16)
            sch.final_wait("g", "gd")
            with nc.Block() as block:
                @block.tensor
                def _(t):
                    sch.replay("t", t, sems)

                @block.vector
                def _(v):
                    sch.replay("v", v, sems)

                @block.scalar
                def _(s):
                    sch.replay("s", s, sems)

                @block.gpsimd
                def _(g):
                    sch.replay("g", g, sems)

                @block.sync
                def _(y):
                    sch.replay("y", y, sems)
            print(f"[kernel] built dbg stage {stage}")
            return nc

        emit_ln()
        if stage == 0.5:
            emit_proj(wk_a, KC, stage_to_dram(qkv_in, D, None))
            emit_proj(wv_a, KC, stage_to_dram(qkv_in, 2 * D, None), wslot_off=2)
            emit_proj(wq_a, KC, stage_to_dram(qkv_in, 0, SCALE_A), wslot_off=1)
            emit_collective(qkv_in, qkv_g, "qkv_a", 3 * D)
            emit_mirror(qkv_g, qkv_m, 3)
            for i in range(12):
                def ddmp(e, i=i):
                    return e.dma_start(out=dbg_t[i * TOK:(i + 1) * TOK, :],
                                       in_=bass.AP(qkv_m, i * TOK * TOK, [[TOK, TOK], [1, TOK]]))
                sch.step("g", ddmp, reads=[("d", "qkv_m", 0)], writes=[("dbg", i)], sem="gd", inc=16)
            for i in range(4):
                def xdmp(e, i=i):
                    return e.dma_start(out=out_t[:, i * 8192:(i + 1) * 8192],
                                       in_=x_sb[:, i * 8192:(i + 1) * 8192])
                sch.step("g", xdmp, reads=[("x", k) for k in range(i * 8, (i + 1) * 8)],
                         writes=[("o", i)], sem="gd", inc=16)
            sch.final_wait("g", "gd")
            with nc.Block() as block:
                @block.tensor
                def _(t):
                    sch.replay("t", t, sems)

                @block.vector
                def _(v):
                    sch.replay("v", v, sems)

                @block.scalar
                def _(s):
                    sch.replay("s", s, sems)

                @block.gpsimd
                def _(g):
                    sch.replay("g", g, sems)

                @block.sync
                def _(y):
                    sch.replay("y", y, sems)
            print(f"[kernel] built dbg stage {stage}")
            return nc

        emit_proj(wk_a, KC, stage_to_dram(qkv_in, D, None))
        emit_proj(wv_a, KC, stage_to_dram(qkv_in, 2 * D, None), wslot_off=2)
        emit_proj(wq_a, KC, stage_to_dram(qkv_in, 0, SCALE_A), wslot_off=1)
        emit_collective(qkv_in, qkv_g, "qkv_a", 3 * D)
        emit_mirror(qkv_g, qkv_m, 3)
        emit_attention("a")
        if stage == 0.8:
            for i in range(8):
                def odmp(e, i=i):
                    return e.dma_start(out=dbg_t[i * 128:(i + 1) * 128, :],
                                       in_=bass.AP(o_in, i * 128 * SEQ, [[SEQ, 128], [1, SEQ]]))
                sch.step("g", odmp, reads=[("d", "o_in", i * 128)], writes=[("dbg", i)],
                         sem="gd", inc=16)
        emit_collective(o_in, o_g, "o_a", HPC * DK)
        emit_mirror_o(o_g)
        emit_outproj(wo_a, o_g)

        if stage >= 2 or stage == 1.5:
            emit_ln()
            emit_ffn()
            if stage == 1.5:
                def dmp1(e):
                    return e.dma_start(out=dbg_t[0:128, 0:32], in_=lgt[:, :])
                sch.step("g", dmp1, reads=[("st", "lgt")], writes=[("dbg", 1)], sem="gd", inc=16)

                def dmp2(e):
                    return e.dma_start(out=dbg_t[0:128, 32:64], in_=gtm[:, :])
                sch.step("g", dmp2, reads=[("st", "gtm")], writes=[("dbg", 2)], sem="gd", inc=16)

                def dmp3(e):
                    return e.dma_start(out=dbg_t[128:132, :], in_=gfm[:, :])
                sch.step("g", dmp3, reads=[("st", "gfm")], writes=[("dbg", 3)], sem="gd", inc=16)

                def dmp4(e):
                    return e.dma_start(out=dbg_t[132:196, :], in_=mid_sb[:, :])
                sch.step("g", dmp4, reads=[("st", "mid")], writes=[("dbg", 4)], sem="gd", inc=16)

                def dmp5(e):
                    return e.dma_start(out=dbg_t[196:260, :], in_=midw[:, :])
                sch.step("g", dmp5, reads=[("st", "midw")], writes=[("dbg", 5)], sem="gd", inc=16)

        if stage >= 3:
            emit_ln()
            def kb_dest(j, ct, pb):
                spar = 0
                cs = ct * 512

                def evac(e, cs=cs, pb=pb):
                    return e.tensor_copy(stg[spar][:, cs:cs + 512], ps[pb][:, :])
                sch.step("v", evac, reads=psp(pb), writes=stg_pages(spar))
                if ct == 1:
                    WID = max(TOK, DK)
                    dma("y", bass.AP(kvb_in, 0, [[WID, 128], [1, TOK]]), stg[spar][:, :],
                        stg_pages(spar), [("d", "kvb_in", 0)])
            emit_proj(wk_b, 1, kb_dest)

            def vb_dest(j, ct, pb):
                spar = 1
                cs = ct * 512

                def evac(e, cs=cs, pb=pb):
                    return e.tensor_copy(stg[spar][:, cs:cs + 512], ps[pb][:, :])
                sch.step("v", evac, reads=psp(pb), writes=stg_pages(spar))
                if ct == 1:
                    def vtr(e):
                        last = None
                        for t8 in range(8):
                            last = e.transpose(psT[:, t8 * 128:(t8 + 1) * 128],
                                               stg[spar][:, t8 * 128:(t8 + 1) * 128],
                                               ident_bf[:, :])
                        return last
                    sch.step("t", vtr, reads=stg_pages(spar) + [("c", "id")],
                             writes=[("ps", "T", 0)], ni=8)

                    def vev(e):
                        return e.tensor_copy(stg[0][:, :], psT[:, :])
                    sch.step("v", vev, reads=[("ps", "T", 0)], writes=stg_pages(0))
                    WID = max(TOK, DK)
                    pairs = []
                    for t8 in range(8):
                        pairs.append((bass.AP(kvb_in, (DK + t8 * 128) * WID, [[WID, 128], [1, DK]]),
                                      stg[0][:, t8 * 128:(t8 + 1) * 128]))
                    dma_multi("y", pairs, stg_pages(0),
                              [("d", "kvb_in", r) for r in range(128, DK + TOK, 128)])
            emit_proj(wv_b, 1, vb_dest, wslot_off=1)
            emit_proj(wq_b, KC, stage_to_dram(q_in_b, 0, SCALE_B), wslot_off=2)
            emit_collective(kvb_in, kvb_g, "kv_b", DK + TOK)
            emit_collective(q_in_b, q_g_b, "q_b", D)
            emit_mirror(q_g_b, qb_m, 1)
            emit_attention("b")
            emit_collective(o_in, o_g, "o_b", HPC * DK)
            emit_mirror_o(o_g)
            emit_outproj(wo_b, o_g)

        if stage >= 4:
            emit_ln()
            emit_ffn()

        for i in range(4):
            dma("g", out_t[:, i * 8192:(i + 1) * 8192], x_sb[:, i * 8192:(i + 1) * 8192],
                [("x", kc_) for kc_ in range(i * 8, (i + 1) * 8)], [("o", i)])
        sch.final_wait("g", "gd")

        with nc.Block() as block:
            @block.tensor
            def _(t):
                sch.replay("t", t, sems)

            @block.vector
            def _(v):
                sch.replay("v", v, sems)

            @block.scalar
            def _(s):
                sch.replay("s", s, sems)

            @block.gpsimd
            def _(g):
                sch.replay("g", g, sems)

            @block.sync
            def _(y):
                sch.replay("y", y, sems)

    print(f"[kernel] built: {sch.n_instr} compute/dma instrs, "
          f"steps t={len(sch.steps['t'])} v={len(sch.steps['v'])} "
          f"s={len(sch.steps['s'])} g={len(sch.steps['g'])} y={len(sch.steps['y'])}")
    return nc



def _prep_w(w, cols=None):
    if cols is not None:
        w = w[:, cols]
    din, dout = w.shape
    nj = dout // 128
    nk = din // 128
    pre = np.ascontiguousarray(
        w.reshape(nk, 128, nj, 128).transpose(2, 1, 0, 3).reshape(nj, 128, din)
        .astype(np.float32))
    return pre


def kernel(L0_attn_a_hidden_states, L0_attn_a_wq, L0_attn_a_wkv, L0_attn_a_wo,
           L0_attn_b_wq, L0_attn_b_wk, L0_attn_b_wv, L0_attn_b_wo,
           L0_ffn_router_w, L0_ffn_A, L0_ffn_B):
    x = np.asarray(L0_attn_a_hidden_states, np.float32)
    H_, DQ, DN, DV = 32, 192, 128, 128

    qcols = np.concatenate([np.arange(h * DQ, h * DQ + DN) for h in range(H_)])
    wq_a = _prep_w(np.asarray(L0_attn_a_wq, np.float32), qcols)
    wkv = np.asarray(L0_attn_a_wkv, np.float32)
    wk_a = _prep_w(wkv[:, :H_ * DN])
    wv_a = _prep_w(wkv[:, H_ * DN:H_ * (DN + DV)])
    wo_a = _prep_w(np.asarray(L0_attn_a_wo, np.float32))
    wq_b = _prep_w(np.asarray(L0_attn_b_wq, np.float32))
    wk_b = _prep_w(np.asarray(L0_attn_b_wk, np.float32))
    wv_b = _prep_w(np.asarray(L0_attn_b_wv, np.float32))
    wo_b = _prep_w(np.asarray(L0_attn_b_wo, np.float32))

    rw = np.asarray(L0_ffn_router_w, np.float32)
    rtr = np.ascontiguousarray(rw.T.reshape(KC, 128, 4).transpose(1, 0, 2).reshape(128, KC * 4))
    A = np.asarray(L0_ffn_A, np.float32).reshape(64, D)
    a_pre = np.ascontiguousarray(A.T.reshape(KC, 128, 64).transpose(1, 0, 2).reshape(128, KC * 64))
    Bw = np.asarray(L0_ffn_B, np.float32)
    b_pre = np.ascontiguousarray(Bw.transpose(0, 2, 1).reshape(64, D))

    ident = np.eye(128, dtype=np.float32)
    ones_c = np.ones((128, 1), np.float32)
    ones_r = np.ones((1, 128), np.float32)
    dmaskv = np.triu(np.full((128, 128), np.float32(MASKV), np.float32), 1)
    eps_c = np.full((128, 1), 1e-5, np.float32)
    e4 = np.zeros((4, 64), np.float32)
    for e_ in range(4):
        e4[e_, e_ * 16:(e_ + 1) * 16] = LORA_SCALE

    shared = dict(wq_a=wq_a, wk_a=wk_a, wv_a=wv_a, wo_a=wo_a, wq_b=wq_b,
                  wk_b=wk_b, wv_b=wv_b, wo_b=wo_b, rtr=rtr, a_pre=a_pre,
                  b_pre=b_pre, ident_in=ident, ones_in=ones_c, onesr_in=ones_r,
                  dmask_in=dmaskv, e4_in=e4, eps_in=eps_c)

    in_maps = []
    for c in range(NCORES):
        b, r = c // RPB, c % RPB
        xs = x[b, r * TOK:(r + 1) * TOK, :]
        xfm = np.ascontiguousarray(
            xs.T.reshape(KC, 128, TOK).transpose(1, 0, 2).reshape(128, KC * TOK))
        in_maps.append({**shared, "x_in": xfm})

    nc = build_nc(STAGE)
    res = run_bass_kernel_spmd(nc, in_maps, core_ids=list(range(NCORES)))

    out = np.empty((BATCH, SEQ, D), np.float32)
    for c in range(NCORES):
        b, r = c // RPB, c % RPB
        o = res.results[c]["out"]
        out[b, r * TOK:(r + 1) * TOK, :] = (
            o.reshape(128, KC, TOK).transpose(2, 1, 0).reshape(TOK, D))
    if STAGE in (0, 0.5, 0.8, 1.5):
        kernel.dbg = [res.results[c].get("dbg") for c in range(NCORES)]
    return out
